# version 19
# baseline (speedup 1.0000x reference)
import math
import contextlib
import numpy as np
import ml_dtypes
import concourse.bass as bass
import concourse.mybir as mybir
from concourse.bass_utils import run_bass_kernel_spmd

F32 = mybir.dt.float32
BF16 = mybir.dt.bfloat16
I32 = mybir.dt.int32
AF = mybir.ActivationFunctionType
ALU = mybir.AluOpType

NCORES = 8
D = 2048
T = 2048
NH = 16
NKV = 4
HD = 128
REP = NH // NKV
I = 5504
NDT = D // 128
NTT = T // 128
NIT = I // 128
ISH = I // NCORES
QC = 512
NQC = T // QC
NHF = 2
NTC = 2
LUTN = 16
EXPC = 6.0
SCALE = 1.0 / math.sqrt(HD)
EPS = 1e-5

QKV = [("k", i) for i in range(NKV)] + [("q", h) for h in range(NH)] + [("v", i) for i in range(NKV)]
NCI = len(QKV)
QKGROUPS = [(ci, hf) for ci in range(NCI) for hf in range(NHF) if QKV[ci][0] in ("q", "k")]

HQ = [(h, qc) for h in range(NH) for qc in range(NQC)]
HQIDX = {v: i for i, v in enumerate(HQ)}
INST = []
for (h, qc) in HQ:
    for kt in range((qc + 1) * QC // 128):
        INST.append((h, qc, kt))
NINST = len(INST)


def lastkt(qc):
    return (qc + 1) * QC // 128 - 1


def inst_geom(h, qc, kt):
    qlo = max(QC * qc, 128 * kt)
    qn = QC * (qc + 1) - qlo
    rel = qlo - QC * qc
    diag = 128 * kt >= QC * qc
    return qlo, qn, rel, diag


_d_rows = []
_r = 0
while _r < ISH:
    if ISH - _r < 128:
        _r = ISH - 128
    _d_rows.append((_r, 128))
    _r += 128
_d_cols = []
_c = 0
while _c < D:
    _d_cols.append((_c, min(ISH, D - _c)))
    _c += min(ISH, D - _c)
DQ = ([("g", t) for t in range(NDT)] + [("u", t) for t in range(NDT)]
      + [("d", (ri, ci)) for ri in range(len(_d_rows)) for ci in range(len(_d_cols))])
NDQ = len(DQ)


def gu_split(it):
    lo, hi = 128 * it, 128 * (it + 1)
    out = []
    for r in range(lo // ISH, (hi - 1) // ISH + 1):
        a, b = max(lo, ISH * r), min(hi, ISH * (r + 1))
        out.append((r, a - ISH * r, b - ISH * r, a - lo))
    return out


DGRP = [(0, 16), (16, 16), (32, 11)]

FBSEQ = []
for _tc in range(NTC):
    for _it in range(NIT):
        for _m in ("g", "u"):
            FBSEQ.append(("gu", _tc, _it, _m))
    for _ot in range(NDT):
        for _gi in range(3):
            FBSEQ.append(("wd", _tc, _ot, _gi))
FBIDX = {v: i for i, v in enumerate(FBSEQ)}


def fb_consume(entry):
    if entry[0] == "gu":
        return ("c_mmg", (entry[1], entry[2], entry[3]))
    return ("c_mmd", (entry[1], entry[2], entry[3]))


def pool3(i):
    return ((2 * i) % 3, (2 * i + 1) % 3)


def gu_banks(g):
    s = (4 * g) % 7
    return [(s + k) % 7 for k in range(4)]


class Ledger:
    def __init__(self):
        self.tot = {}
        self.ev = {}

    def inc(self, sem, key, n=1):
        self.tot[sem] = self.tot.get(sem, 0) + n
        assert (sem, key) not in self.ev, (sem, key)
        self.ev[(sem, key)] = self.tot[sem]

    def at(self, sem, key):
        return self.ev[(sem, key)]


def plan():
    L = Ledger()
    for i in range(5):
        L.inc("d_cst", i, 16)
    for i in range(NDT):
        L.inc("d_x", i, 16)
    for ci in range(NCI):
        for h2 in range(2):
            L.inc("d_w", ("qkv", ci, h2), 16)
    for hf in range(NHF):
        for ct in range(NDT):
            for h2 in range(2):
                L.inc("d_w", ("p", hf, ct, h2), 16)
            L.inc("d_xr", (hf, ct), 16)
    for tc in range(NTC):
        for it in range(NIT):
            for m in ("g", "u"):
                L.inc("d_gu", (tc, it, m), 16 * len(gu_split(it)))
        for ot in range(NDT):
            for gi in range(3):
                L.inc("d_wd", (tc, ot, gi), 16)
            L.inc("d_x2r", (tc, ot), 16)
        for ot in range(NDT):
            L.inc("d_out", (tc, ot), 16)
    for key in DQ:
        L.inc("d_idx", key, 16)
        L.inc("d_wdeq", key, 16)
    for hf in range(NHF):
        for ct in range(NDT):
            L.inc("d_r1w", (hf, ct), 16)
        for ct in range(NDT):
            L.inc("d_r1r", (hf, ct), 16)
            L.inc("d_x2w", (hf, ct), 16)
    for tc in range(NTC):
        for ot in range(NDT):
            L.inc("d_r2w", (tc, ot), 16)
        for ot in range(NDT):
            L.inc("d_r2r", (tc, ot), 16)
    L.inc("d_ccg", 0, 1)
    L.inc("d_ccd", 0, 1)
    for ii, (h, qc, kt) in enumerate(INST):
        if inst_geom(h, qc, kt)[3]:
            L.inc("c_dg", ii)
    for key in DQ:
        L.inc("c_idxc", key)
    for i in range(NDT):
        L.inc("c_xbf", i)
    for ci in range(NCI):
        for h2 in range(2):
            L.inc("c_wcvt", ("qkv", ci, h2))
        for hf in range(NHF):
            L.inc("c_evq", (ci, hf))
            if QKV[ci][0] == "v":
                L.inc("c_vn", (ci, hf))
    for ii in range(NINST):
        L.inc("c_ex", ii)
    for hqi in range(len(HQ)):
        L.inc("c_rcp", hqi)
    for hf in range(NHF):
        for ct in range(NDT):
            for h2 in range(2):
                L.inc("c_wcvt", ("p", hf, ct, h2))
            L.inc("c_rb", (hf, ct))
        L.inc("c_mrs", hf)
        for ct in range(NDT):
            L.inc("c_x2b", (hf, ct))
    for tc in range(NTC):
        for it in range(NIT):
            L.inc("c_sil", (tc, it))
        for ot in range(NDT):
            L.inc("c_rb2", (tc, ot))
        L.inc("c_mr2s", tc)
    for ci in range(NCI):
        for hf in range(NHF):
            L.inc("c_mmqk", (ci, hf))
            if QKV[ci][0] == "v":
                L.inc("c_mmqk", ("vt", ci, hf))
    for ii in range(NINST):
        L.inc("c_sc", ii)
        L.inc("c_pv", ii)
    for hf in range(NHF):
        for ct in range(NDT):
            L.inc("c_mmp", (hf, ct))
            L.inc("c_mst", (hf, ct))
    for tc in range(NTC):
        for it in range(NIT):
            for m in ("g", "u"):
                L.inc("c_mmg", (tc, it, m))
        for ot in range(NDT):
            for gi in range(3):
                L.inc("c_mmd", (tc, ot, gi))
            L.inc("c_mst2", (tc, ot))
    for g in QKGROUPS:
        L.inc("c_rope", g)
    for key in DQ:
        L.inc("c_deq", key)
    for hqi in range(len(HQ)):
        L.inc("c_ev", hqi)
    for hf in range(NHF):
        for ct in range(NDT):
            L.inc("c_rv", (hf, ct, 0))
            L.inc("c_rv", (hf, ct, 1))
        L.inc("c_mrv", hf)
        for ct in range(NDT):
            L.inc("c_nv", (hf, ct))
    for tc in range(NTC):
        for it in range(NIT):
            L.inc("c_h", (tc, it))
        for ot in range(NDT):
            L.inc("c_rv2", (tc, ot, 0))
            L.inc("c_rv2", (tc, ot, 1))
        L.inc("c_mr2v", tc)
        for ot in range(NDT):
            L.inc("c_nv2", (tc, ot))
    return L


SEMS = ["d_cst", "d_x", "d_w", "d_xr", "d_gu", "d_wd", "d_x2r", "d_out",
        "d_idx", "d_wdeq", "d_r1w", "d_r1r", "d_x2w", "d_r2w", "d_r2r",
        "d_ccg", "d_ccd",
        "c_idxc", "c_xbf", "c_wcvt", "c_evq", "c_vn", "c_ex", "c_rcp", "c_rb",
        "c_mrs", "c_x2b", "c_sil", "c_rb2", "c_mr2s",
        "c_mmqk", "c_sc", "c_pv", "c_mmp", "c_mst", "c_mmg", "c_mmd", "c_mst2",
        "c_rope", "c_deq", "c_dg", "c_ev", "c_rv", "c_mrv", "c_nv",
        "c_h", "c_rv2", "c_mr2v", "c_nv2"]


def build_nc():
    L = plan()
    emitted = {s: 0 for s in SEMS}

    nc = bass.Bass("TRN2", target_bir_lowering=False, debug=False,
                   dynamic_dma_scratch_size=16384)

    xT = nc.dram_tensor("xT", [D, T], F32, kind="ExternalInput").ap()
    wq = nc.dram_tensor("wq", [D, D], F32, kind="ExternalInput").ap()
    wk = nc.dram_tensor("wk", [D, NKV * HD], F32, kind="ExternalInput").ap()
    wv = nc.dram_tensor("wv", [D, NKV * HD], F32, kind="ExternalInput").ap()
    wp = nc.dram_tensor("wp", [D, D], F32, kind="ExternalInput").ap()
    lutr = nc.dram_tensor("lutr", [128, LUTN], F32, kind="ExternalInput").ap()
    gidx = nc.dram_tensor("gidx", [D, ISH], I32, kind="ExternalInput").ap()
    uidx = nc.dram_tensor("uidx", [D, ISH], I32, kind="ExternalInput").ap()
    didx = nc.dram_tensor("didx", [ISH, D], I32, kind="ExternalInput").ap()
    cosd = nc.dram_tensor("cosd", [128, T], F32, kind="ExternalInput").ap()
    sind = nc.dram_tensor("sind", [128, T], F32, kind="ExternalInput").ap()
    cstd = nc.dram_tensor("cstd", [128, 4 * 128], BF16, kind="ExternalInput").ap()
    actc = nc.dram_tensor("actc", [128, 4], F32, kind="ExternalInput").ap()
    outT = nc.dram_tensor("outT", [D, T], F32, kind="ExternalOutput").ap()

    ccgu_in = nc.dram_tensor("ccgu_in", [D, 2 * ISH], BF16, kind="Internal").ap()
    ccd_in = nc.dram_tensor("ccd_in", [ISH, D], BF16, kind="Internal").ap()
    ccgu_out = nc.dram_tensor("ccgu_out", [NCORES * D, 2 * ISH], BF16, kind="Internal").ap()
    ccd_out = nc.dram_tensor("ccd_out", [I, D], BF16, kind="Internal").ap()
    r1s = nc.dram_tensor("r1s", [D, T], F32, kind="Internal").ap()
    x2s = nc.dram_tensor("x2s", [D, T], F32, kind="Internal").ap()
    r2s = nc.dram_tensor("r2s", [D, T], F32, kind="Internal").ap()
    if DEBUG_DUMP:
        dmp_r1 = nc.dram_tensor("dmp_r1", [D, T], F32, kind="ExternalOutput").ap()
        dmp_x2 = nc.dram_tensor("dmp_x2", [D, T], F32, kind="ExternalOutput").ap()
        dmp_r2 = nc.dram_tensor("dmp_r2", [D, T], F32, kind="ExternalOutput").ap()
        dmp_g = nc.dram_tensor("dmp_g", [NCORES * D, ISH], BF16, kind="ExternalOutput").ap()
        dmp_d = nc.dram_tensor("dmp_d", [I, D], BF16, kind="ExternalOutput").ap()
        dmp_din = nc.dram_tensor("dmp_din", [ISH, D], BF16, kind="ExternalOutput").ap()

    es = contextlib.ExitStack()
    with es:
        sbt = lambda name, shape, dt: es.enter_context(nc.sbuf_tensor(name, shape, dt))
        XB = [sbt(f"XB{i}", [128, T], BF16) for i in range(NDT)]
        QY = [sbt(f"QY{i}", [128, T], BF16) for i in range(NH)]
        KTB = [sbt(f"KT{i}", [128, T], BF16) for i in range(NKV)]
        VB = [sbt(f"V{i}", [128, NKV * HD], BF16) for i in range(NTT)]
        MB = sbt("MB", [128, T], F32)
        RSB = sbt("RSB", [128, T], F32)
        AR = [sbt("AR0", [128, 1024], F32), sbt("AR1", [128, T], F32)]
        WBF = [sbt(f"WBF{i}", [128, D], BF16) for i in range(2)]
        POOL = sbt("POOL", [128, 4096], BF16)
        IDX32 = sbt("IDX32", [128, ISH], I32)
        LUT = sbt("LUT", [128, LUTN], F32)
        CST = sbt("CST", [128, 4 * 128], BF16)
        ACTC = sbt("ACTC", [128, 4], F32)
        nc.const_aps.register(F32, -EXPC, ACTC[:, 0:1])
        nc.const_aps.register(F32, SCALE, ACTC[:, 1:2])
        nc.const_aps.register(F32, -1.0, ACTC[:, 2:3])
        MAIN = es.enter_context(nc.psum_tensor("MAIN", [128, 3584], F32))
        AUX = es.enter_context(nc.psum_tensor("AUX", [128, 1024], BF16))

        MASK01 = CST[:, 0:128]
        ONES = CST[:, 128:256]
        IDENT = CST[:, 256:384]
        OON = CST[:, 384:512]

        PTv = [POOL[:, 0:512], POOL[:, 512:1024]]
        VSTG = POOL[:, 0:1024]
        IDXBF = [POOL[:, 1024:1024 + ISH], POOL[:, 3088:3088 + ISH]]
        DQT = POOL[:, 1712:1712 + ISH]
        WDQ = [POOL[:, 2400:2400 + ISH], POOL[:, 2400:2400 + ISH]]
        FBv = [POOL[:, 0:2048], POOL[:, 2048:4096]]
        RBv = KTB[0][:, 0:1024]
        RSQv = KTB[0][:, 1024:2048]
        T1M = KTB[1][:, 0:1024]
        RB2v = WBF[0][:, 0:1024]
        RSQ2v = WBF[0][:, 1024:2048]
        T1M2 = WBF[1][:, 0:1024]
        RCv = [AR[0][:, 0:512], AR[0][:, 512:1024], None]
        QY14F = QY[14].bitcast(F32)
        QY15F = QY[15].bitcast(F32)
        KTB2F = KTB[2].bitcast(F32)

        def wslot_qkv(k):
            if k % 2 == 0:
                return AR[0][:, :]
            return QY15F[:, :] if k <= 41 else MB[:, 0:1024]

        def wslot_p(k):
            return AR[0][:, :] if k % 2 == 0 else KTB2F[:, :]

        S = {s: es.enter_context(nc.semaphore(s)) for s in SEMS}
        block = es.enter_context(nc.Block())

        def BK(b, cols=512):
            return MAIN[:, 512 * b:512 * b + cols]

        CURENG = ["?"]

        def WAIT(eng, s, key):
            if TRACE is not None:
                TRACE.append((CURENG[0], "wait", s, L.at(s, key)))
            eng.wait_ge(S[s], L.at(s, key))

        def WTOT(eng, s):
            if TRACE is not None:
                TRACE.append((CURENG[0], "wait", s, L.tot[s]))
            eng.wait_ge(S[s], L.tot[s])

        def INC(instr, s, key, n=1):
            if TRACE is not None:
                TRACE.append((CURENG[0], "inc", s, n))
            emitted[s] += n
            assert L.ev[(s, key)] == emitted[s], (s, key, L.ev[(s, key)], emitted[s])
            instr.then_inc(S[s], n)

        def RAWINC(instr, s, n=16):
            if TRACE is not None:
                TRACE.append((CURENG[0], "inc", s, n))
            emitted[s] += n
            instr.then_inc(S[s], n)

        def INCF(instr, s, key, n=16):
            if TRACE is not None:
                TRACE.append((CURENG[0], "inc", s, n))
            emitted[s] += n
            assert L.ev[(s, key)] == emitted[s], (s, key, L.ev[(s, key)], emitted[s])
            instr.then_inc(S[s], n)

        PIECES = []
        for j in range(NH):
            for c in range(4):
                PIECES.append((QY[j], 512 * c))
        for j in range(NKV):
            for c in range(4):
                PIECES.append((KTB[j], 512 * c))
        for j in range(NTT):
            PIECES.append((VB[j], 0))

        def hpiece(it, c):
            t, off = PIECES[2 * it + c]
            return t[:, off:off + 512]

        def qkv_bank(gi):
            return 0 if gi % 2 == 0 else 2

        @block.sync
        def _(sync):
            CURENG[0] = "sync"
            INC(sync.dma_start(out=LUT[:, :], in_=lutr[:, :]), "d_cst", 0, 16)
            INC(sync.dma_start(out=CST[:, :], in_=cstd[:, :]), "d_cst", 1, 16)
            INC(sync.dma_start(out=MB[:, :], in_=cosd[:, :]), "d_cst", 2, 16)
            INC(sync.dma_start(out=RSB[:, :], in_=sind[:, :]), "d_cst", 3, 16)
            INC(sync.dma_start(out=ACTC[:, :], in_=actc[:, :]), "d_cst", 4, 16)
            for i in range(NDT):
                if i >= 1:
                    WAIT(sync, "c_xbf", i - 1)
                d = sync.dma_start(out=AR[1][:, :], in_=xT[128 * i:128 * (i + 1), :])
                INC(d, "d_x", i, 16)
            for ci, (kind, idx) in enumerate(QKV):
                src = {"q": wq, "k": wk, "v": wv}[kind]
                for h2 in range(2):
                    if (ci, h2) != (0, 0):
                        prevh = ("qkv", ci, h2 - 1) if h2 == 1 else ("qkv", ci - 1, 1)
                        WAIT(sync, "c_wcvt", prevh)
                    d = sync.dma_start(
                        out=AR[0][:, :].rearrange("p (t c) -> p t c", t=NDT // 2),
                        in_=src.rearrange("(t p) c -> p t c", p=128)[
                            :, 8 * h2:8 * (h2 + 1), 128 * idx:128 * (idx + 1)])
                    INC(d, "d_w", ("qkv", ci, h2), 16)
            for hf in range(NHF):
                for ct in range(NDT):
                    if hf == 0 and ct == 0:
                        WTOT(sync, "c_ev")
                    for h2 in range(2):
                        prevh = (("p", hf, ct, 0) if h2 == 1 else
                                 (("p", hf, ct - 1, 1) if ct > 0 else
                                  (("p", hf - 1, NDT - 1, 1) if hf > 0 else ("qkv", NCI - 1, 1))))
                        WAIT(sync, "c_wcvt", prevh)
                        d = sync.dma_start(
                            out=AR[0][:, :].rearrange("p (t c) -> p t c", t=NDT // 2),
                            in_=wp.rearrange("(t p) c -> p t c", p=128)[
                                :, 8 * h2:8 * (h2 + 1), 128 * ct:128 * (ct + 1)])
                        INC(d, "d_w", ("p", hf, ct, h2), 16)
                    if (hf, ct) != (0, 0):
                        pc = (hf, ct - 1) if ct > 0 else (hf - 1, NDT - 1)
                        WAIT(sync, "c_rb", pc)
                        WAIT(sync, "d_r1w", pc)
                    if ct == 0 and hf >= 1:
                        WAIT(sync, "c_mrs", hf - 1)
                    d = sync.dma_start(out=AR[1][:, 0:1024],
                                       in_=xT[128 * ct:128 * (ct + 1), 1024 * hf:1024 * (hf + 1)])
                    INC(d, "d_xr", (hf, ct), 16)
            for tc in range(NTC):
                for ot in range(NDT):
                    if ot >= 1:
                        WAIT(sync, "c_rb2", (tc, ot - 1))
                        WAIT(sync, "d_r2w", (tc, ot - 1))
                    elif tc == 0:
                        WAIT(sync, "c_mrs", 1)
                        WAIT(sync, "c_rb", (1, NDT - 1))
                        WAIT(sync, "d_r1w", (1, NDT - 1))
                    else:
                        WAIT(sync, "c_mr2s", tc - 1)
                        WAIT(sync, "c_rb2", (tc - 1, NDT - 1))
                        WAIT(sync, "d_r2w", (tc - 1, NDT - 1))
                    WAIT(sync, "d_x2w", (tc, ot))
                    d = sync.dma_start(out=AR[1][:, 0:1024],
                                       in_=x2s[128 * ot:128 * (ot + 1), 1024 * tc:1024 * (tc + 1)])
                    INC(d, "d_x2r", (tc, ot), 16)
            WTOT(sync, "d_out")
            if DEBUG_DUMP:
                dsem = S["d_cst"]
                base = L.tot["d_cst"]
                pairs = [(dmp_r1, r1s), (dmp_x2, x2s), (dmp_r2, r2s),
                         (dmp_g, ccgu_out[:, 0:ISH]), (dmp_d, ccd_out), (dmp_din, ccd_in)]
                for k2, (dst2, src2) in enumerate(pairs):
                    sync.dma_start(out=dst2[:, :], in_=src2[:, :]).then_inc(dsem, 16)
                sync.wait_ge(dsem, base + 16 * len(pairs))

        @block.scalar
        def _(scalar):
            CURENG[0] = "scalar"
            scalar.wait_ge(S["d_cst"], L.tot["d_cst"])
            for i in range(NDT):
                WAIT(scalar, "d_x", i)
                c = scalar.copy(out=XB[i][:, :], in_=AR[1][:, :])
                INC(c, "c_xbf", i)
            def wcvt_qkv(ci):
                for h2 in range(2):
                    WAIT(scalar, "d_w", ("qkv", ci, h2))
                    if ci >= 2 and h2 == 0:
                        WAIT(scalar, "c_mmqk", (ci - 2, NHF - 1))
                    c = scalar.copy(out=WBF[ci % 2][:, 1024 * h2:1024 * (h2 + 1)],
                                    in_=AR[0][:, :])
                    INC(c, "c_wcvt", ("qkv", ci, h2))

            wcvt_qkv(0)
            for ci, (kind, idx) in enumerate(QKV):
                if ci + 1 < NCI:
                    wcvt_qkv(ci + 1)
                for hf in range(NHF):
                    gi = ci * NHF + hf
                    b0 = qkv_bank(gi)
                    WAIT(scalar, "c_mmqk", (ci, hf))
                    if kind in ("q", "k"):
                        g2 = QKGROUPS.index((ci, hf))
                        if g2 >= 1:
                            WAIT(scalar, "c_rope", QKGROUPS[g2 - 1])
                        scalar.copy(out=AR[1][:, 0:1024], in_=BK(b0, 1024))
                        scalar.copy(out=AR[1][0:64, 1024:2048], in_=AR[1][64:128, 0:1024])
                        c = scalar.copy(out=AR[1][64:128, 1024:2048], in_=AR[1][0:64, 0:1024])
                        INC(c, "c_evq", (ci, hf))
                    else:
                        vgi = (ci - (NCI - NKV)) * NHF + hf
                        if vgi >= 1:
                            pci = NCI - NKV + (vgi - 1) // NHF
                            WAIT(scalar, "c_mmqk", ("vt", pci, (vgi - 1) % NHF))
                        c = scalar.copy(out=VSTG, in_=BK(b0, 1024))
                        INC(c, "c_evq", (ci, hf))
                        WAIT(scalar, "c_mmqk", ("vt", ci, hf))
                        for b in range(8):
                            tt = hf * 8 + b
                            c = scalar.copy(out=VB[tt][:, 128 * idx:128 * (idx + 1)],
                                            in_=AUX[:, 128 * b:128 * (b + 1)])
                        INC(c, "c_vn", (ci, hf))
            for ii, (h, qc, kt) in enumerate(INST):
                qlo, qn, rel, diag = inst_geom(h, qc, kt)
                hqi = HQIDX[(h, qc)]
                sl = ii % 2
                WAIT(scalar, "c_sc", ii)
                if ii >= 2:
                    WAIT(scalar, "c_pv", ii - 2)
                c = scalar.activation(out=PTv[sl][:, 0:qn], in_=BK(ii % 3)[:, 0:qn],
                                      func=AF.Exp, bias=-EXPC, scale=SCALE)
                INC(c, "c_ex", ii)
                if kt == lastkt(qc):
                    WAIT(scalar, "c_pv", ii)
                    if hqi >= 2:
                        WAIT(scalar, "c_ev", hqi - 2)
                    scalar.activation(out=MB[:, 0:512], in_=BK(3 + hqi % 2), func=AF.Ln)
                    c = scalar.activation(out=RCv[hqi % 2], in_=MB[:, 0:512],
                                          func=AF.Exp, scale=-1.0)
                    INC(c, "c_rcp", hqi)

            def wcvt_proj(hf, ct):
                gpi = NCI + hf * NDT + ct
                for h2 in range(2):
                    WAIT(scalar, "d_w", ("p", hf, ct, h2))
                    if h2 == 0:
                        prev2 = gpi - 2
                        if prev2 >= NCI:
                            hf2, ct2 = divmod(prev2 - NCI, NDT)
                            WAIT(scalar, "c_mmp", (hf2, ct2))
                        else:
                            WAIT(scalar, "c_mmqk", (prev2, NHF - 1))
                    c = scalar.copy(out=WBF[gpi % 2][:, 1024 * h2:1024 * (h2 + 1)],
                                    in_=AR[0][:, :])
                    INC(c, "c_wcvt", ("p", hf, ct, h2))

            def proj_ct(hf, ct):
                nxt = (hf, ct + 1) if ct + 1 < NDT else ((hf + 1, 0) if hf + 1 < NHF else None)
                if nxt is not None:
                    wcvt_proj(*nxt)
                WAIT(scalar, "c_rv", (hf, ct, 1))
                if (hf, ct) != (0, 0):
                    pc = (hf, ct - 1) if ct > 0 else (hf - 1, NDT - 1)
                    WAIT(scalar, "c_mst", pc)
                if hf == 1 and ct == 0:
                    WAIT(scalar, "c_mrv", 0)
                rt = AR[1][:, 0:1024]
                scalar.copy(out=RBv, in_=rt)
                c = scalar.activation(out=RSQv, in_=RBv, func=AF.Square)
                INC(c, "c_rb", (hf, ct))
                d = scalar.dma_start(out=r1s[128 * ct:128 * (ct + 1), 1024 * hf:1024 * (hf + 1)],
                                     in_=rt)
                INC(d, "d_r1w", (hf, ct), 16)

            def norm1_ct(hf, ct):
                if ct >= 1:
                    WAIT(scalar, "d_x2w", (hf, ct - 1))
                    WAIT(scalar, "c_nv", (hf, ct - 1))
                WAIT(scalar, "d_r1w", (hf, ct))
                if ct == 0:
                    WAIT(scalar, "c_mrv", hf)
                d = scalar.dma_start(out=AR[1][:, 1024:2048],
                                     in_=r1s[128 * ct:128 * (ct + 1), 1024 * hf:1024 * (hf + 1)])
                INC(d, "d_r1r", (hf, ct), 16)
                WAIT(scalar, "c_nv", (hf, ct))
                c = scalar.copy(out=XB[ct][:, 1024 * hf:1024 * (hf + 1)],
                                in_=AR[1][:, 1024:2048])
                INC(c, "c_x2b", (hf, ct))
                d = scalar.dma_start(out=x2s[128 * ct:128 * (ct + 1), 1024 * hf:1024 * (hf + 1)],
                                     in_=AR[1][:, 1024:2048])
                INC(d, "d_x2w", (hf, ct), 16)

            wcvt_proj(0, 0)
            for ct in range(NDT):
                proj_ct(0, ct)
            WAIT(scalar, "c_mrv", 0)
            c = scalar.activation(out=RSB[:, 0:1024], in_=AR[1][:, 0:1024], func=AF.Sqrt)
            INC(c, "c_mrs", 0)
            for ct in range(NDT):
                proj_ct(1, ct)
                norm1_ct(0, ct)
            WAIT(scalar, "c_mrv", 1)
            c = scalar.activation(out=RSB[:, 1024:2048], in_=AR[1][:, 0:1024], func=AF.Sqrt)
            INC(c, "c_mrs", 1)

            def sil_it(tc, it):
                WAIT(scalar, "c_mmg", (tc, it, "g"))
                w = gu_banks(tc * NIT + it)
                scalar.activation(out=hpiece(it, 0), in_=BK(w[0]), func=AF.Silu)
                c = scalar.activation(out=hpiece(it, 1), in_=BK(w[1]), func=AF.Silu)
                INC(c, "c_sil", (tc, it))

            def down_ot(tc, ot):
                WAIT(scalar, "c_rv2", (tc, ot, 1))
                if tc == 0 and ot == 0:
                    WAIT(scalar, "c_mmp", (1, NDT - 1))
                else:
                    p1 = (tc, ot - 1) if ot >= 1 else (tc - 1, NDT - 1)
                    WAIT(scalar, "c_mst2", p1)
                if tc == 1 and ot == 0:
                    WAIT(scalar, "c_mr2v", 0)
                r2t = AR[1][:, 0:1024]
                scalar.copy(out=RB2v, in_=r2t)
                c = scalar.activation(out=RSQ2v, in_=RB2v, func=AF.Square)
                INC(c, "c_rb2", (tc, ot))
                d = scalar.dma_start(out=r2s[128 * ot:128 * (ot + 1), 1024 * tc:1024 * (tc + 1)],
                                     in_=r2t)
                INC(d, "d_r2w", (tc, ot), 16)

            def out_dma(tc, ot):
                WAIT(scalar, "c_nv2", (tc, ot))
                d = scalar.dma_start(
                    out=outT[128 * ot:128 * (ot + 1), 1024 * tc:1024 * (tc + 1)],
                    in_=AR[1][:, 1024:2048])
                INC(d, "d_out", (tc, ot), 16)

            def norm2_dma(tc, ot):
                if ot >= 1:
                    out_dma(tc, ot - 1)
                    WAIT(scalar, "d_out", (tc, ot - 1))
                else:
                    WAIT(scalar, "c_mr2s", tc)
                    if tc == 0:
                        WAIT(scalar, "d_x2w", (1, NDT - 1))
                        WAIT(scalar, "c_nv", (1, NDT - 1))
                    else:
                        WAIT(scalar, "d_out", (0, NDT - 1))
                WAIT(scalar, "d_r2w", (tc, ot))
                d = scalar.dma_start(out=AR[1][:, 1024:2048],
                                     in_=r2s[128 * ot:128 * (ot + 1), 1024 * tc:1024 * (tc + 1)])
                INC(d, "d_r2r", (tc, ot), 16)

            for it in range(NIT):
                sil_it(0, it)
                if it < NDT:
                    norm1_ct(1, it)
            for ot in range(NDT):
                down_ot(0, ot)
            WAIT(scalar, "c_mr2v", 0)
            c = scalar.activation(out=RSB[:, 0:1024], in_=AR[1][:, 0:1024], func=AF.Sqrt)
            INC(c, "c_mr2s", 0)
            for it in range(NIT):
                sil_it(1, it)
                if it < NDT:
                    norm2_dma(0, it)
                elif it == NDT:
                    out_dma(0, NDT - 1)
            for ot in range(NDT):
                down_ot(1, ot)
            WAIT(scalar, "c_mr2v", 1)
            c = scalar.activation(out=RSB[:, 1024:2048], in_=AR[1][:, 0:1024], func=AF.Sqrt)
            INC(c, "c_mr2s", 1)
            for ot in range(NDT):
                norm2_dma(1, ot)
            out_dma(1, NDT - 1)

        @block.tensor
        def _(tensor):
            CURENG[0] = "tensor"
            for ci, (kind, idx) in enumerate(QKV):
                for hf in range(NHF):
                    gi = ci * NHF + hf
                    b0 = qkv_bank(gi)
                    if gi == 0:
                        WAIT(tensor, "c_xbf", NDT - 1)
                    WAIT(tensor, "c_wcvt", ("qkv", ci, 0 if hf == 0 else 1))
                    if gi >= 2:
                        WAIT(tensor, "c_evq", ((gi - 2) // NHF, (gi - 2) % NHF))
                    last = None
                    for dt in range(NDT):
                        if dt == 8 and hf == 0:
                            WAIT(tensor, "c_wcvt", ("qkv", ci, 1))
                        for c in range(2):
                            last = tensor.matmul(
                                BK(b0 + c),
                                WBF[ci % 2][:, 128 * dt:128 * (dt + 1)],
                                XB[dt][:, 1024 * hf + 512 * c:1024 * hf + 512 * (c + 1)],
                                start=(dt == 0), stop=(dt == NDT - 1))
                    INC(last, "c_mmqk", (ci, hf))
                    if kind == "v":
                        WAIT(tensor, "c_evq", (ci, hf))
                        vgi = (ci - (NCI - NKV)) * NHF + hf
                        if vgi >= 1:
                            pci = NCI - NKV + (vgi - 1) // NHF
                            WAIT(tensor, "c_vn", (pci, (vgi - 1) % NHF))
                        last = None
                        for b in range(8):
                            last = tensor.transpose(AUX[:, 128 * b:128 * (b + 1)],
                                                    VSTG[:, 128 * b:128 * (b + 1)],
                                                    IDENT)
                        INC(last, "c_mmqk", ("vt", ci, hf))

            def emit_scores(ii):
                h, qc, kt = INST[ii]
                qlo, qn, rel, diag = inst_geom(h, qc, kt)
                kv = h // REP
                if ii == 0:
                    WTOT(tensor, "c_rope")
                    WTOT(tensor, "c_vn")
                    WTOT(tensor, "c_evq")
                if ii >= 3:
                    WAIT(tensor, "c_ex", ii - 3)
                mmi = tensor.matmul(BK(ii % 3)[:, 0:qn],
                                    KTB[kv][:, 128 * kt:128 * (kt + 1)],
                                    QY[h][:, qlo:qlo + qn], start=True, stop=True)
                INC(mmi, "c_sc", ii)

            def emit_sumspv(ii):
                h, qc, kt = INST[ii]
                qlo, qn, rel, diag = inst_geom(h, qc, kt)
                kv = h // REP
                hqi = HQIDX[(h, qc)]
                sl = ii % 2
                lk = lastkt(qc)
                if diag:
                    WAIT(tensor, "c_dg", ii)
                else:
                    WAIT(tensor, "c_ex", ii)
                if kt == 0 and hqi >= 2:
                    WAIT(tensor, "c_rcp", hqi - 2)
                    WAIT(tensor, "c_ev", hqi - 2)
                tensor.matmul(BK(3 + hqi % 2)[:, rel:rel + qn], ONES, PTv[sl][:, 0:qn],
                              start=(kt == 0), stop=(kt == lk))
                mmi = tensor.matmul(BK(5 + hqi % 2)[:, rel:rel + qn],
                                    VB[kt][:, 128 * kv:128 * (kv + 1)], PTv[sl][:, 0:qn],
                                    start=(kt == 0), stop=(kt == lk))
                INC(mmi, "c_pv", ii)

            prev = None
            for ii in range(NINST):
                emit_scores(ii)
                if prev is not None:
                    emit_sumspv(prev)
                prev = ii
            emit_sumspv(prev)

            for hf in range(NHF):
                for ct in range(NDT):
                    i = hf * NDT + ct
                    c0, c1 = pool3(i)
                    if i == 0:
                        WTOT(tensor, "c_ex")
                        WTOT(tensor, "c_rcp")
                        WTOT(tensor, "c_ev")
                        WTOT(tensor, "c_deq")
                    WAIT(tensor, "c_wcvt", ("p", hf, ct, 0))
                    if i >= 2:
                        hf2, ct2 = divmod(i - 2, NDT)
                        WAIT(tensor, "c_rv", (hf2, ct2, 1))
                    if i >= 1:
                        hf1, ct1 = divmod(i - 1, NDT)
                        WAIT(tensor, "c_rv", (hf1, ct1, 0))
                    gpi = NCI + i
                    last = None
                    for dt in range(NDT):
                        if dt == 8:
                            WAIT(tensor, "c_wcvt", ("p", hf, ct, 1))
                        last = tensor.matmul(BK(c0), WBF[gpi % 2][:, 128 * dt:128 * (dt + 1)],
                                             QY[dt][:, 1024 * hf:1024 * hf + 512],
                                             start=(dt == 0), stop=(dt == NDT - 1))
                        last = tensor.matmul(BK(c1), WBF[gpi % 2][:, 128 * dt:128 * (dt + 1)],
                                             QY[dt][:, 1024 * hf + 512:1024 * (hf + 1)],
                                             start=(dt == 0), stop=(dt == NDT - 1))
                    INC(last, "c_mmp", (hf, ct))
                    WAIT(tensor, "c_rb", (hf, ct))
                    if hf == 1 and ct == 0:
                        WAIT(tensor, "c_mrv", 0)
                    tensor.matmul(BK(3), OON, RBv[:, 0:512],
                                  start=(ct == 0), stop=(ct == NDT - 1))
                    tensor.matmul(BK(4), OON, RBv[:, 512:1024],
                                  start=(ct == 0), stop=(ct == NDT - 1))
                    tensor.matmul(BK(5), OON, RSQv[:, 0:512],
                                  start=(ct == 0), stop=(ct == NDT - 1))
                    last = tensor.matmul(BK(6), OON, RSQv[:, 512:1024],
                                         start=(ct == 0), stop=(ct == NDT - 1))
                    INC(last, "c_mst", (hf, ct))

            for tc in range(NTC):
                for it in range(NIT):
                    g2 = tc * NIT + it
                    w = gu_banks(g2)
                    if g2 == 0:
                        WAIT(tensor, "c_x2b", (0, NDT - 1))
                        WAIT(tensor, "c_mrv", 1)
                        WAIT(tensor, "c_rv", (1, NDT - 1, 1))
                    if tc == 1 and it == 0:
                        WAIT(tensor, "c_x2b", (1, NDT - 1))
                        WAIT(tensor, "c_mr2v", 0)
                        WAIT(tensor, "c_rv2", (0, NDT - 1, 1))
                    if g2 >= 2:
                        tc2, it2 = divmod(g2 - 2, NIT)
                        WAIT(tensor, "c_h", (tc2, it2))
                    for mi, m in enumerate(("g", "u")):
                        WAIT(tensor, "d_gu", (tc, it, m))
                        if m == "u" and g2 >= 1:
                            tc1, it1 = divmod(g2 - 1, NIT)
                            WAIT(tensor, "c_sil", (tc1, it1))
                        fbk = FBIDX[("gu", tc, it, m)]
                        last = None
                        for dt in range(NDT):
                            for c in range(2):
                                last = tensor.matmul(
                                    BK(w[2 * mi + c]),
                                    FBv[fbk % 2][:, 128 * dt:128 * (dt + 1)],
                                    XB[dt][:, 1024 * tc + 512 * c:1024 * tc + 512 * (c + 1)],
                                    start=(dt == 0), stop=(dt == NDT - 1))
                        INC(last, "c_mmg", (tc, it, m))
                for ot in range(NDT):
                    j = tc * NDT + ot
                    c0, c1 = pool3(j)
                    if ot == 0:
                        WAIT(tensor, "c_h", (tc, NIT - 1))
                        WAIT(tensor, "c_sil", (tc, NIT - 1))
                    if j >= 2:
                        tc2, ot2 = divmod(j - 2, NDT)
                        WAIT(tensor, "c_rv2", (tc2, ot2, 1))
                    if j >= 1:
                        tc1, ot1 = divmod(j - 1, NDT)
                        WAIT(tensor, "c_rv2", (tc1, ot1, 0))
                    for gi, (g0, gn) in enumerate(DGRP):
                        WAIT(tensor, "d_wd", (tc, ot, gi))
                        fbk = FBIDX[("wd", tc, ot, gi)]
                        last = None
                        for itl in range(gn):
                            it = g0 + itl
                            last = tensor.matmul(BK(c0), FBv[fbk % 2][:, 128 * itl:128 * (itl + 1)],
                                                 hpiece(it, 0),
                                                 start=(it == 0), stop=(it == NIT - 1))
                            last = tensor.matmul(BK(c1), FBv[fbk % 2][:, 128 * itl:128 * (itl + 1)],
                                                 hpiece(it, 1),
                                                 start=(it == 0), stop=(it == NIT - 1))
                        INC(last, "c_mmd", (tc, ot, gi))
                    WAIT(tensor, "c_rb2", (tc, ot))
                    if tc == 1 and ot == 0:
                        WAIT(tensor, "c_mr2v", 0)
                    tensor.matmul(BK(3), OON, RB2v[:, 0:512],
                                  start=(ot == 0), stop=(ot == NDT - 1))
                    tensor.matmul(BK(4), OON, RB2v[:, 512:1024],
                                  start=(ot == 0), stop=(ot == NDT - 1))
                    tensor.matmul(BK(5), OON, RSQ2v[:, 0:512],
                                  start=(ot == 0), stop=(ot == NDT - 1))
                    last = tensor.matmul(BK(6), OON, RSQ2v[:, 512:1024],
                                         start=(ot == 0), stop=(ot == NDT - 1))
                    INC(last, "c_mst2", (tc, ot))

        @block.vector
        def _(vector):
            CURENG[0] = "vector"
            st = {"deq": 0}

            def deq_tile():
                mi = st["deq"]
                if mi >= NDQ:
                    return
                m, dt = DQ[mi]
                WAIT(vector, "c_idxc", (m, dt))
                if mi >= 1:
                    WAIT(vector, "d_wdeq", DQ[mi - 1])
                if m == "d":
                    rn = _d_rows[dt[0]][1]
                    cn = _d_cols[dt[1]][1]
                else:
                    rn, cn = 128, ISH
                acc = WDQ[mi % 2]
                v = None
                for j in range(LUTN):
                    dst = acc if j == 0 else DQT
                    v2 = vector.tensor_scalar(out=dst[0:rn, 0:cn], in0=IDXBF[mi % 2][0:rn, 0:cn],
                                              scalar1=float(j), scalar2=LUT[0:rn, j:j + 1],
                                              op0=ALU.is_equal, op1=ALU.mult)
                    if j == 0:
                        v = v2
                    else:
                        v = vector.tensor_add(out=acc[0:rn, 0:cn], in0=acc[0:rn, 0:cn],
                                              in1=DQT[0:rn, 0:cn])
                INC(v, "c_deq", (m, dt))
                st["deq"] += 1

            WTOT(vector, "d_cst")
            for _ in range(10):
                deq_tile()
            for g2, (ci, hf) in enumerate(QKGROUPS):
                kind, idx = QKV[ci]
                WAIT(vector, "c_evq", (ci, hf))
                tgt = QY[idx] if kind == "q" else KTB[idx]
                vector.tensor_mul(out=AR[1][:, 0:1024], in0=AR[1][:, 0:1024],
                                  in1=MB[:, 1024 * hf:1024 * (hf + 1)])
                vector.tensor_mul(out=AR[1][:, 1024:2048], in0=AR[1][:, 1024:2048],
                                  in1=RSB[:, 1024 * hf:1024 * (hf + 1)])
                v = vector.tensor_add(out=tgt[:, 1024 * hf:1024 * (hf + 1)],
                                      in0=AR[1][:, 0:1024], in1=AR[1][:, 1024:2048])
                INC(v, "c_rope", (ci, hf))
                deq_tile()
            for ii, (h, qc, kt) in enumerate(INST):
                qlo, qn, rel, diag = inst_geom(h, qc, kt)
                hqi = HQIDX[(h, qc)]
                if diag:
                    WAIT(vector, "c_ex", ii)
                    v = vector.tensor_mul(out=PTv[ii % 2][:, 0:128], in0=PTv[ii % 2][:, 0:128],
                                          in1=MASK01)
                    INC(v, "c_dg", ii)
                if kt == lastkt(qc):
                    deq_tile()
                    WAIT(vector, "c_rcp", hqi)
                    v = vector.tensor_mul(out=QY[h][:, QC * qc:QC * (qc + 1)],
                                          in0=BK(5 + hqi % 2), in1=RCv[hqi % 2])
                    INC(v, "c_ev", hqi)
            while st["deq"] < NDQ:
                deq_tile()

            def rv_ct(hf, ct):
                i = hf * NDT + ct
                c0, c1 = pool3(i)
                WAIT(vector, "c_mmp", (hf, ct))
                WAIT(vector, "d_xr", (hf, ct))
                rt = AR[1][:, 0:1024]
                v = vector.tensor_add(out=rt[:, 0:512], in0=rt[:, 0:512], in1=BK(c0))
                INC(v, "c_rv", (hf, ct, 0))
                v = vector.tensor_add(out=rt[:, 512:1024], in0=rt[:, 512:1024], in1=BK(c1))
                INC(v, "c_rv", (hf, ct, 1))

            def mchain(cols, s1b, s2b, t1m, dwsem, dwkey, sem, key, pre=()):
                for (ps, pk) in pre:
                    WAIT(vector, ps, pk)
                vector.tensor_copy(out=MB[:, cols], in_=BK(s1b, 1024))
                vector.tensor_mul(out=t1m, in0=MB[:, cols], in1=MB[:, cols])
                vector.scalar_tensor_tensor(out=AR[1][:, 1024:2048], in0=BK(s2b, 1024),
                                            scalar=EPS, in1=t1m,
                                            op0=ALU.add, op1=ALU.subtract)
                WAIT(vector, dwsem, dwkey)
                v = vector.reciprocal(out=AR[1][:, 0:1024], in_=AR[1][:, 1024:2048])
                INC(v, sem, key)

            def nv_ct(hf, ct):
                WAIT(vector, "d_r1r", (hf, ct))
                if ct == 0:
                    WAIT(vector, "c_mrs", hf)
                slt = AR[1][:, 1024:2048]
                vector.tensor_sub(out=slt, in0=slt, in1=MB[:, 1024 * hf:1024 * (hf + 1)])
                v = vector.tensor_mul(out=slt, in0=slt, in1=RSB[:, 1024 * hf:1024 * (hf + 1)])
                INC(v, "c_nv", (hf, ct))

            for ct in range(NDT):
                rv_ct(0, ct)
            WAIT(vector, "c_mst", (0, NDT - 1))
            WTOT(vector, "d_wdeq")
            mchain(slice(0, 1024), 3, 5, T1M, "d_r1w", (0, NDT - 1), "c_mrv", 0)
            for ct in range(NDT):
                rv_ct(1, ct)
                nv_ct(0, ct)
            WAIT(vector, "c_mst", (1, NDT - 1))
            mchain(slice(1024, 2048), 3, 5, T1M, "d_r1w", (1, NDT - 1), "c_mrv", 1,
                   pre=[("c_x2b", (0, NDT - 1)), ("d_x2w", (0, NDT - 1))])

            def hmul(tc, it):
                g2 = tc * NIT + it
                w = gu_banks(g2)
                WAIT(vector, "c_sil", (tc, it))
                WAIT(vector, "c_mmg", (tc, it, "u"))
                if tc == 1 and it == 0:
                    WAIT(vector, "c_mmd", (0, NDT - 1, 2))
                vector.tensor_mul(out=hpiece(it, 0), in0=hpiece(it, 0), in1=BK(w[2]))
                v = vector.tensor_mul(out=hpiece(it, 1), in0=hpiece(it, 1), in1=BK(w[3]))
                INC(v, "c_h", (tc, it))

            def rv2_ot(tc, ot):
                j = tc * NDT + ot
                c0, c1 = pool3(j)
                WAIT(vector, "c_mmd", (tc, ot, 2))
                WAIT(vector, "d_x2r", (tc, ot))
                r2t = AR[1][:, 0:1024]
                v = vector.tensor_add(out=r2t[:, 0:512], in0=r2t[:, 0:512], in1=BK(c0))
                INC(v, "c_rv2", (tc, ot, 0))
                v = vector.tensor_add(out=r2t[:, 512:1024], in0=r2t[:, 512:1024], in1=BK(c1))
                INC(v, "c_rv2", (tc, ot, 1))

            def nv2_ot(tc, ot):
                WAIT(vector, "d_r2r", (tc, ot))
                if ot == 0:
                    WAIT(vector, "c_mr2s", tc)
                slt = AR[1][:, 1024:2048]
                vector.tensor_sub(out=slt, in0=slt, in1=MB[:, 1024 * tc:1024 * (tc + 1)])
                v = vector.tensor_mul(out=slt, in0=slt, in1=RSB[:, 1024 * tc:1024 * (tc + 1)])
                INC(v, "c_nv2", (tc, ot))

            for it in range(NIT):
                hmul(0, it)
                if it < NDT:
                    nv_ct(1, it)
            for ot in range(NDT):
                rv2_ot(0, ot)
            WAIT(vector, "c_mst2", (0, NDT - 1))
            WAIT(vector, "c_mmp", (1, NDT - 1))
            mchain(slice(0, 1024), 3, 5, T1M2, "d_r2w", (0, NDT - 1), "c_mr2v", 0)
            for it in range(NIT):
                hmul(1, it)
                if it < NDT:
                    nv2_ot(0, it)
            for ot in range(NDT):
                rv2_ot(1, ot)
            WAIT(vector, "c_mst2", (1, NDT - 1))
            mchain(slice(1024, 2048), 3, 5, T1M2, "d_r2w", (1, NDT - 1), "c_mr2v", 1,
                   pre=[("d_out", (0, NDT - 1))])
            for ot in range(NDT):
                nv2_ot(1, ot)

        @block.gpsimd
        def _(gpsimd):
            CURENG[0] = "gpsimd"
            st = {"conv": 0, "dma": 0, "out": 0}

            def idx_dma():
                mi = st["dma"]
                m, dt = DQ[mi]
                if mi >= 1:
                    WAIT(gpsimd, "c_idxc", DQ[mi - 1])
                if m == "d":
                    ri, cj = dt
                    r0, rn = _d_rows[ri]
                    c0, cn = _d_cols[cj]
                    d = gpsimd.dma_start(out=IDX32[0:rn, 0:cn], in_=didx[r0:r0 + rn, c0:c0 + cn])
                else:
                    srct = gidx if m == "g" else uidx
                    d = gpsimd.dma_start(out=IDX32[:, :], in_=srct[128 * dt:128 * (dt + 1), :])
                INC(d, "d_idx", (m, dt), 16)
                st["dma"] += 1

            def idx_conv():
                mi = st["conv"]
                m, dt = DQ[mi]
                WAIT(gpsimd, "d_idx", (m, dt))
                if mi >= 2:
                    WAIT(gpsimd, "c_deq", DQ[mi - 2])
                if m == "d":
                    rn = _d_rows[dt[0]][1]
                    cn = _d_cols[dt[1]][1]
                    c = gpsimd.tensor_copy(out=IDXBF[mi % 2][0:rn, 0:cn], in_=IDX32[0:rn, 0:cn])
                else:
                    c = gpsimd.tensor_copy(out=IDXBF[mi % 2][:, :], in_=IDX32[:, :])
                INC(c, "c_idxc", (m, dt))
                st["conv"] += 1

            def wdeq_dma():
                mi = st["out"]
                m, dt = DQ[mi]
                WAIT(gpsimd, "c_deq", (m, dt))
                if m == "d":
                    ri, cj = dt
                    r0, rn = _d_rows[ri]
                    c0, cn = _d_cols[cj]
                    d = gpsimd.dma_start(out=ccd_in[r0:r0 + rn, c0:c0 + cn],
                                         in_=WDQ[mi % 2][0:rn, 0:cn])
                else:
                    co = 0 if m == "g" else ISH
                    d = gpsimd.dma_start(out=ccgu_in[128 * dt:128 * (dt + 1), co:co + ISH],
                                         in_=WDQ[mi % 2][:, :])
                INC(d, "d_wdeq", (m, dt), 16)
                st["out"] += 1

            def deq_step():
                if st["dma"] == 0:
                    idx_dma()
                if st["conv"] < NDQ:
                    idx_conv()
                if st["dma"] < NDQ:
                    idx_dma()
                if st["out"] < st["conv"] - 1:
                    wdeq_dma()

            while st["conv"] < 38:
                deq_step()
            for ii, (h, qc, kt) in enumerate(INST):
                qlo, qn, rel, diag = inst_geom(h, qc, kt)
                if kt == lastkt(qc) and (st["conv"] < NDQ or st["out"] < st["conv"] - 1):
                    deq_step()
            while st["conv"] < NDQ:
                deq_step()
            while st["out"] < NDQ:
                wdeq_dma()
            WAIT(gpsimd, "d_wdeq", DQ[2 * NDT - 1])
            cc = gpsimd.collective_compute(
                "AllGather", ALU.bypass, replica_groups=[list(range(NCORES))],
                ins=[ccgu_in[:, :]], outs=[ccgu_out[:, :]])
            INC(cc, "d_ccg", 0, 1)
            WAIT(gpsimd, "d_wdeq", DQ[NDQ - 1])
            cc = gpsimd.collective_compute(
                "AllGather", ALU.bypass, replica_groups=[list(range(NCORES))],
                ins=[ccd_in[:, :]], outs=[ccd_out[:, :]])
            INC(cc, "d_ccd", 0, 1)
            for tc in range(NTC):
                for it in range(NIT):
                    for m in ("g", "u"):
                        fbk = FBIDX[("gu", tc, it, m)]
                        if fbk >= 2:
                            cs, ck = fb_consume(FBSEQ[fbk - 2])
                            WAIT(gpsimd, cs, ck)
                        if tc == 0 and it == 0:
                            gpsimd.wait_ge(S["d_ccg"], 1)
                            if m == "g":
                                WTOT(gpsimd, "c_pv")
                                WTOT(gpsimd, "c_ev")
                                WTOT(gpsimd, "d_wdeq")
                        off = 0 if m == "g" else ISH
                        srcr = ccgu_out.rearrange("(r t p) c -> r p t c", r=NCORES, p=128)
                        dstr = FBv[fbk % 2].rearrange("p (t c) -> p t c", t=NDT)
                        parts = gu_split(it)
                        for pi2, (blk, c0, c1, doff) in enumerate(parts):
                            d = gpsimd.dma_start(out=dstr[:, :, doff:doff + (c1 - c0)],
                                                 in_=srcr[blk][:, :, off + c0:off + c1])
                            if pi2 < len(parts) - 1:
                                RAWINC(d, "d_gu", 16)
                            else:
                                INCF(d, "d_gu", (tc, it, m), 16)
                for ot in range(NDT):
                    for gi, (g0, gn) in enumerate(DGRP):
                        fbk = FBIDX[("wd", tc, ot, gi)]
                        cs, ck = fb_consume(FBSEQ[fbk - 2])
                        WAIT(gpsimd, cs, ck)
                        if tc == 0 and ot == 0 and gi == 0:
                            gpsimd.wait_ge(S["d_ccd"], 1)
                        srcr = ccd_out.rearrange("(t p) c -> p t c", p=128)
                        d = gpsimd.dma_start(
                            out=FBv[fbk % 2][:, 0:128 * gn].rearrange("p (t c) -> p t c", t=gn),
                            in_=srcr[:, g0:g0 + gn, 128 * ot:128 * (ot + 1)])
                        INC(d, "d_wd", (tc, ot, gi), 16)

    for s in SEMS:
        assert emitted[s] == L.tot[s], (s, emitted[s], L.tot[s])
    return nc


def _rope_tables():
    dd = np.arange(0, HD, 2, dtype=np.float64) / HD
    inv = 1.0 / np.power(10000.0, dd)
    t = np.arange(T, dtype=np.float64)
    fr = inv[:, None] * t[None, :]
    cosT = np.concatenate([np.cos(fr), np.cos(fr)], axis=0).astype(np.float32)
    sinT = np.concatenate([-np.sin(fr), np.sin(fr)], axis=0).astype(np.float32)
    return np.ascontiguousarray(cosT), np.ascontiguousarray(sinT)


def _consts():
    kl = np.arange(128)[:, None]
    ql = np.arange(128)[None, :]
    mask01 = (ql >= kl).astype(ml_dtypes.bfloat16)
    ones = np.ones((128, 128), dtype=ml_dtypes.bfloat16)
    ident = np.eye(128, dtype=ml_dtypes.bfloat16)
    oon = np.full((128, 128), 1.0 / D, dtype=ml_dtypes.bfloat16)
    return np.ascontiguousarray(np.concatenate([mask01, ones, ident, oon], axis=1))


_NC_CACHE = None
TRACE = None
DEBUG_DUMP = False


def kernel(L0_attn_x, L0_attn_Wq, L0_attn_Wk, L0_attn_Wv, L0_attn_Wproj,
           L0_ffn_lut, L0_ffn_gate_idx, L0_ffn_up_idx, L0_ffn_down_idx):
    global _NC_CACHE
    x = np.asarray(L0_attn_x, dtype=np.float32)
    wqh = np.ascontiguousarray(np.asarray(L0_attn_Wq, dtype=np.float32))
    wkh = np.ascontiguousarray(np.asarray(L0_attn_Wk, dtype=np.float32))
    wvh = np.ascontiguousarray(np.asarray(L0_attn_Wv, dtype=np.float32))
    wph = np.ascontiguousarray(np.asarray(L0_attn_Wproj, dtype=np.float32))
    lut = np.asarray(L0_ffn_lut, dtype=np.float32)
    gi = np.asarray(L0_ffn_gate_idx, dtype=np.int32)
    ui = np.asarray(L0_ffn_up_idx, dtype=np.int32)
    di = np.asarray(L0_ffn_down_idx, dtype=np.int32)

    lutr = np.ascontiguousarray(np.broadcast_to(lut[None, :], (128, LUTN)))
    cosT, sinT = _rope_tables()
    cst = _consts()
    giT = np.ascontiguousarray(gi.T)
    uiT = np.ascontiguousarray(ui.T)
    diT = np.ascontiguousarray(di.T)

    if _NC_CACHE is None:
        _NC_CACHE = build_nc()
    nc = _NC_CACHE

    in_maps = []
    for b in range(NCORES):
        in_maps.append({
            "xT": np.ascontiguousarray(x[b].T),
            "wq": wqh, "wk": wkh, "wv": wvh, "wp": wph,
            "lutr": lutr,
            "gidx": np.ascontiguousarray(giT[:, ISH * b:ISH * (b + 1)]),
            "uidx": np.ascontiguousarray(uiT[:, ISH * b:ISH * (b + 1)]),
            "didx": np.ascontiguousarray(diT[ISH * b:ISH * (b + 1), :]),
            "cosd": cosT, "sind": sinT, "cstd": cst,
            "actc": np.ascontiguousarray(np.broadcast_to(
                np.array([-EXPC, SCALE, -1.0, 0.0], dtype=np.float32)[None, :], (128, 4))),
        })
    res = run_bass_kernel_spmd(nc, in_maps, core_ids=list(range(NCORES)))
    out = np.empty((NCORES, T, D), dtype=np.float32)
    for b in range(NCORES):
        out[b] = res.results[b]["outT"].T
    return out


# version 20
# speedup vs baseline: 1.0180x; 1.0180x over previous
import math
import contextlib
import numpy as np
import ml_dtypes
import concourse.bass as bass
import concourse.mybir as mybir
from concourse.bass_utils import run_bass_kernel_spmd

F32 = mybir.dt.float32
BF16 = mybir.dt.bfloat16
I32 = mybir.dt.int32
AF = mybir.ActivationFunctionType
ALU = mybir.AluOpType

NCORES = 8
D = 2048
T = 2048
NH = 16
NKV = 4
HD = 128
REP = NH // NKV
I = 5504
NDT = D // 128
NTT = T // 128
NIT = I // 128
ISH = I // NCORES
QC = 512
NQC = T // QC
NHF = 2
NTC = 2
LUTN = 16
EXPC = 6.0
SCALE = 1.0 / math.sqrt(HD)
EPS = 1e-5

QKV = [("k", i) for i in range(NKV)] + [("q", h) for h in range(NH)] + [("v", i) for i in range(NKV)]
NCI = len(QKV)
QKGROUPS = [(ci, hf) for ci in range(NCI) for hf in range(NHF) if QKV[ci][0] in ("q", "k")]

HQ = [(h, qc) for h in range(NH) for qc in range(NQC)]
HQIDX = {v: i for i, v in enumerate(HQ)}
INST = []
for (h, qc) in HQ:
    for kt in range((qc + 1) * QC // 128):
        INST.append((h, qc, kt))
NINST = len(INST)


def lastkt(qc):
    return (qc + 1) * QC // 128 - 1


def inst_geom(h, qc, kt):
    qlo = max(QC * qc, 128 * kt)
    qn = QC * (qc + 1) - qlo
    rel = qlo - QC * qc
    diag = 128 * kt >= QC * qc
    return qlo, qn, rel, diag


_d_rows = []
_r = 0
while _r < ISH:
    if ISH - _r < 128:
        _r = ISH - 128
    _d_rows.append((_r, 128))
    _r += 128
_d_cols = []
_c = 0
while _c < D:
    _d_cols.append((_c, min(ISH, D - _c)))
    _c += min(ISH, D - _c)
DQ = ([("g", t) for t in range(NDT)] + [("u", t) for t in range(NDT)]
      + [("d", (ri, ci)) for ri in range(len(_d_rows)) for ci in range(len(_d_cols))])
NDQ = len(DQ)


def gu_split(it):
    lo, hi = 128 * it, 128 * (it + 1)
    out = []
    for r in range(lo // ISH, (hi - 1) // ISH + 1):
        a, b = max(lo, ISH * r), min(hi, ISH * (r + 1))
        out.append((r, a - ISH * r, b - ISH * r, a - lo))
    return out


DGRP = [(0, 16), (16, 16), (32, 11)]

FBSEQ = []
for _tc in range(NTC):
    for _it in range(NIT):
        for _m in ("g", "u"):
            FBSEQ.append(("gu", _tc, _it, _m))
    for _ot in range(NDT):
        for _gi in range(3):
            FBSEQ.append(("wd", _tc, _ot, _gi))
FBIDX = {v: i for i, v in enumerate(FBSEQ)}


def fb_consume(entry):
    if entry[0] == "gu":
        return ("c_mmg", (entry[1], entry[2], entry[3]))
    return ("c_mmd", (entry[1], entry[2], entry[3]))


def pool3(i):
    return ((2 * i) % 3, (2 * i + 1) % 3)


def gu_banks(g):
    s = (4 * g) % 7
    return [(s + k) % 7 for k in range(4)]


class Ledger:
    def __init__(self):
        self.tot = {}
        self.ev = {}

    def inc(self, sem, key, n=1):
        self.tot[sem] = self.tot.get(sem, 0) + n
        assert (sem, key) not in self.ev, (sem, key)
        self.ev[(sem, key)] = self.tot[sem]

    def at(self, sem, key):
        return self.ev[(sem, key)]


def plan():
    L = Ledger()
    for i in range(5):
        L.inc("d_cst", i, 16)
    for i in range(NDT):
        L.inc("d_x", i, 16 if i % 2 == 0 else 32)
    for ci in range(NCI):
        for h2 in range(2):
            L.inc("d_w", ("qkv", ci, h2), 16)
    for hf in range(NHF):
        for ct in range(NDT):
            for h2 in range(2):
                L.inc("d_w", ("p", hf, ct, h2), 16)
            L.inc("d_xr", (hf, ct), 16)
    for tc in range(NTC):
        for it in range(NIT):
            for m in ("g", "u"):
                L.inc("d_gu", (tc, it, m), 16 * len(gu_split(it)))
        for ot in range(NDT):
            for gi in range(3):
                L.inc("d_wd", (tc, ot, gi), 16)
            L.inc("d_x2r", (tc, ot), 16)
        for ot in range(NDT):
            L.inc("d_out", (tc, ot), 16)
    for key in DQ:
        L.inc("d_idx", key, 16)
        L.inc("d_wdeq", key, 16)
    for hf in range(NHF):
        for ct in range(NDT):
            L.inc("d_r1w", (hf, ct), 16)
        for ct in range(NDT):
            L.inc("d_r1r", (hf, ct), 16)
            L.inc("d_x2w", (hf, ct), 16)
    for tc in range(NTC):
        for ot in range(NDT):
            L.inc("d_r2w", (tc, ot), 16)
        for ot in range(NDT):
            L.inc("d_r2r", (tc, ot), 16)
    L.inc("d_ccg", 0, 1)
    L.inc("d_ccd", 0, 1)
    for ii, (h, qc, kt) in enumerate(INST):
        if inst_geom(h, qc, kt)[3]:
            L.inc("c_dg", ii)
    for key in DQ:
        L.inc("c_idxc", key)
    for i in range(NDT):
        L.inc("c_xbf", i)
    for ci in range(NCI):
        for h2 in range(2):
            L.inc("c_wcvt", ("qkv", ci, h2))
        for hf in range(NHF):
            L.inc("c_evq", (ci, hf))
            if QKV[ci][0] == "v":
                L.inc("c_vn", (ci, hf))
    for ii in range(NINST):
        L.inc("c_ex", ii)
    for hqi in range(len(HQ)):
        L.inc("c_rcp", hqi)
    for hf in range(NHF):
        for ct in range(NDT):
            for h2 in range(2):
                L.inc("c_wcvt", ("p", hf, ct, h2))
            L.inc("c_rb", (hf, ct))
        L.inc("c_mrs", hf)
        for ct in range(NDT):
            L.inc("c_x2b", (hf, ct))
    for tc in range(NTC):
        for it in range(NIT):
            L.inc("c_sil", (tc, it))
        for ot in range(NDT):
            L.inc("c_rb2", (tc, ot))
        L.inc("c_mr2s", tc)
    for ci in range(NCI):
        for hf in range(NHF):
            L.inc("c_mmqk", (ci, hf))
            if QKV[ci][0] == "v":
                L.inc("c_mmqk", ("vt", ci, hf))
    for ii in range(NINST):
        L.inc("c_sc", ii)
        L.inc("c_pv", ii)
    for hf in range(NHF):
        for ct in range(NDT):
            L.inc("c_mmp", (hf, ct))
            L.inc("c_mst", (hf, ct))
    for tc in range(NTC):
        for it in range(NIT):
            for m in ("g", "u"):
                L.inc("c_mmg", (tc, it, m))
        for ot in range(NDT):
            for gi in range(3):
                L.inc("c_mmd", (tc, ot, gi))
            L.inc("c_mst2", (tc, ot))
    for g in QKGROUPS:
        L.inc("c_rope", g)
    for key in DQ:
        L.inc("c_deq", key)
    for hqi in range(len(HQ)):
        L.inc("c_ev", hqi)
    for hf in range(NHF):
        for ct in range(NDT):
            L.inc("c_rv", (hf, ct, 0))
            L.inc("c_rv", (hf, ct, 1))
        L.inc("c_mrv", hf)
        for ct in range(NDT):
            L.inc("c_nv", (hf, ct))
    for tc in range(NTC):
        for it in range(NIT):
            L.inc("c_h", (tc, it))
        for ot in range(NDT):
            L.inc("c_rv2", (tc, ot, 0))
            L.inc("c_rv2", (tc, ot, 1))
        L.inc("c_mr2v", tc)
        for ot in range(NDT):
            L.inc("c_nv2", (tc, ot))
    return L


SEMS = ["d_cst", "d_x", "d_w", "d_xr", "d_gu", "d_wd", "d_x2r", "d_out",
        "d_idx", "d_wdeq", "d_r1w", "d_r1r", "d_x2w", "d_r2w", "d_r2r",
        "d_ccg", "d_ccd",
        "c_idxc", "c_xbf", "c_wcvt", "c_evq", "c_vn", "c_ex", "c_rcp", "c_rb",
        "c_mrs", "c_x2b", "c_sil", "c_rb2", "c_mr2s",
        "c_mmqk", "c_sc", "c_pv", "c_mmp", "c_mst", "c_mmg", "c_mmd", "c_mst2",
        "c_rope", "c_deq", "c_dg", "c_ev", "c_rv", "c_mrv", "c_nv",
        "c_h", "c_rv2", "c_mr2v", "c_nv2"]


def build_nc():
    L = plan()
    emitted = {s: 0 for s in SEMS}

    nc = bass.Bass("TRN2", target_bir_lowering=False, debug=False,
                   dynamic_dma_scratch_size=16384)

    xT = nc.dram_tensor("xT", [D, T], F32, kind="ExternalInput").ap()
    wq = nc.dram_tensor("wq", [D, D], F32, kind="ExternalInput").ap()
    wk = nc.dram_tensor("wk", [D, NKV * HD], F32, kind="ExternalInput").ap()
    wv = nc.dram_tensor("wv", [D, NKV * HD], F32, kind="ExternalInput").ap()
    wp = nc.dram_tensor("wp", [D, D], F32, kind="ExternalInput").ap()
    lutr = nc.dram_tensor("lutr", [128, LUTN], F32, kind="ExternalInput").ap()
    gidx = nc.dram_tensor("gidx", [D, ISH], I32, kind="ExternalInput").ap()
    uidx = nc.dram_tensor("uidx", [D, ISH], I32, kind="ExternalInput").ap()
    didx = nc.dram_tensor("didx", [ISH, D], I32, kind="ExternalInput").ap()
    cosd = nc.dram_tensor("cosd", [128, T], F32, kind="ExternalInput").ap()
    sind = nc.dram_tensor("sind", [128, T], F32, kind="ExternalInput").ap()
    cstd = nc.dram_tensor("cstd", [128, 4 * 128], BF16, kind="ExternalInput").ap()
    actc = nc.dram_tensor("actc", [128, 4], F32, kind="ExternalInput").ap()
    outT = nc.dram_tensor("outT", [D, T], F32, kind="ExternalOutput").ap()

    ccgu_in = nc.dram_tensor("ccgu_in", [D, 2 * ISH], BF16, kind="Internal").ap()
    ccd_in = nc.dram_tensor("ccd_in", [ISH, D], BF16, kind="Internal").ap()
    ccgu_out = nc.dram_tensor("ccgu_out", [NCORES * D, 2 * ISH], BF16, kind="Internal").ap()
    ccd_out = nc.dram_tensor("ccd_out", [I, D], BF16, kind="Internal").ap()
    r1s = nc.dram_tensor("r1s", [D, T], F32, kind="Internal").ap()
    x2s = nc.dram_tensor("x2s", [D, T], F32, kind="Internal").ap()
    r2s = nc.dram_tensor("r2s", [D, T], F32, kind="Internal").ap()
    if DEBUG_DUMP:
        dmp_r1 = nc.dram_tensor("dmp_r1", [D, T], F32, kind="ExternalOutput").ap()
        dmp_x2 = nc.dram_tensor("dmp_x2", [D, T], F32, kind="ExternalOutput").ap()
        dmp_r2 = nc.dram_tensor("dmp_r2", [D, T], F32, kind="ExternalOutput").ap()
        dmp_g = nc.dram_tensor("dmp_g", [NCORES * D, ISH], BF16, kind="ExternalOutput").ap()
        dmp_d = nc.dram_tensor("dmp_d", [I, D], BF16, kind="ExternalOutput").ap()
        dmp_din = nc.dram_tensor("dmp_din", [ISH, D], BF16, kind="ExternalOutput").ap()

    es = contextlib.ExitStack()
    with es:
        sbt = lambda name, shape, dt: es.enter_context(nc.sbuf_tensor(name, shape, dt))
        XB = [sbt(f"XB{i}", [128, T], BF16) for i in range(NDT)]
        QY = [sbt(f"QY{i}", [128, T], BF16) for i in range(NH)]
        KTB = [sbt(f"KT{i}", [128, T], BF16) for i in range(NKV)]
        VB = [sbt(f"V{i}", [128, NKV * HD], BF16) for i in range(NTT)]
        MB = sbt("MB", [128, T], F32)
        RSB = sbt("RSB", [128, T], F32)
        AR = [sbt("AR0", [128, 1024], F32), sbt("AR1", [128, T], F32)]
        WBF = [sbt(f"WBF{i}", [128, D], BF16) for i in range(2)]
        POOL = sbt("POOL", [128, 4096], BF16)
        IDX32 = sbt("IDX32", [128, ISH], I32)
        LUT = sbt("LUT", [128, LUTN], F32)
        CST = sbt("CST", [128, 4 * 128], BF16)
        ACTC = sbt("ACTC", [128, 4], F32)
        nc.const_aps.register(F32, -EXPC, ACTC[:, 0:1])
        nc.const_aps.register(F32, SCALE, ACTC[:, 1:2])
        nc.const_aps.register(F32, -1.0, ACTC[:, 2:3])
        MAIN = es.enter_context(nc.psum_tensor("MAIN", [128, 3584], F32))
        AUX = es.enter_context(nc.psum_tensor("AUX", [128, 1024], BF16))

        MASK01 = CST[:, 0:128]
        ONES = CST[:, 128:256]
        IDENT = CST[:, 256:384]
        OON = CST[:, 384:512]

        PTv = [POOL[:, 0:512], POOL[:, 512:1024]]
        VSTG = POOL[:, 0:1024]
        IDXBF = [POOL[:, 1024:1024 + ISH], POOL[:, 3088:3088 + ISH]]
        DQT = POOL[:, 1712:1712 + ISH]
        WDQ = [POOL[:, 2400:2400 + ISH], POOL[:, 2400:2400 + ISH]]
        FBv = [POOL[:, 0:2048], POOL[:, 2048:4096]]
        RBv = KTB[0][:, 0:1024]
        RSQv = KTB[0][:, 1024:2048]
        T1M = KTB[1][:, 0:1024]
        RB2v = WBF[0][:, 0:1024]
        RSQ2v = WBF[0][:, 1024:2048]
        T1M2 = WBF[1][:, 0:1024]
        RCv = [AR[0][:, 0:512], AR[0][:, 512:1024], None]
        QY14F = QY[14].bitcast(F32)
        QY15F = QY[15].bitcast(F32)

        S = {s: es.enter_context(nc.semaphore(s)) for s in SEMS}
        block = es.enter_context(nc.Block())

        def BK(b, cols=512):
            return MAIN[:, 512 * b:512 * b + cols]

        CURENG = ["?"]

        def WAIT(eng, s, key):
            if TRACE is not None:
                TRACE.append((CURENG[0], "wait", s, L.at(s, key)))
            eng.wait_ge(S[s], L.at(s, key))

        def WTOT(eng, s):
            if TRACE is not None:
                TRACE.append((CURENG[0], "wait", s, L.tot[s]))
            eng.wait_ge(S[s], L.tot[s])

        def INC(instr, s, key, n=1):
            if TRACE is not None:
                TRACE.append((CURENG[0], "inc", s, n))
            emitted[s] += n
            assert L.ev[(s, key)] == emitted[s], (s, key, L.ev[(s, key)], emitted[s])
            instr.then_inc(S[s], n)

        def RAWINC(instr, s, n=16):
            if TRACE is not None:
                TRACE.append((CURENG[0], "inc", s, n))
            emitted[s] += n
            instr.then_inc(S[s], n)

        def INCF(instr, s, key, n=16):
            if TRACE is not None:
                TRACE.append((CURENG[0], "inc", s, n))
            emitted[s] += n
            assert L.ev[(s, key)] == emitted[s], (s, key, L.ev[(s, key)], emitted[s])
            instr.then_inc(S[s], n)

        PIECES = []
        for j in range(NH):
            for c in range(4):
                PIECES.append((QY[j], 512 * c))
        for j in range(NKV):
            for c in range(4):
                PIECES.append((KTB[j], 512 * c))
        for j in range(NTT):
            PIECES.append((VB[j], 0))

        def hpiece(it, c):
            t, off = PIECES[2 * it + c]
            return t[:, off:off + 512]

        def qkv_bank(gi):
            return 0 if gi % 2 == 0 else 2

        @block.sync
        def _(sync):
            CURENG[0] = "sync"
            INC(sync.dma_start(out=LUT[:, :], in_=lutr[:, :]), "d_cst", 0, 16)
            INC(sync.dma_start(out=CST[:, :], in_=cstd[:, :]), "d_cst", 1, 16)
            INC(sync.dma_start(out=MB[:, :], in_=cosd[:, :]), "d_cst", 2, 16)
            INC(sync.dma_start(out=RSB[:, :], in_=sind[:, :]), "d_cst", 3, 16)
            INC(sync.dma_start(out=ACTC[:, :], in_=actc[:, :]), "d_cst", 4, 16)
            for i in range(NDT):
                if i >= 2:
                    WAIT(sync, "c_xbf", i - 2)
                if i % 2 == 0:
                    d = sync.dma_start(out=AR[1][:, :], in_=xT[128 * i:128 * (i + 1), :])
                    INC(d, "d_x", i, 16)
                else:
                    d = sync.dma_start(out=QY14F[:, :],
                                       in_=xT[128 * i:128 * (i + 1), 0:1024])
                    RAWINC(d, "d_x", 16)
                    d = sync.dma_start(out=QY15F[:, :],
                                       in_=xT[128 * i:128 * (i + 1), 1024:2048])
                    INCF(d, "d_x", i, 16)
            for ci, (kind, idx) in enumerate(QKV):
                src = {"q": wq, "k": wk, "v": wv}[kind]
                for h2 in range(2):
                    if (ci, h2) != (0, 0):
                        prevh = ("qkv", ci, h2 - 1) if h2 == 1 else ("qkv", ci - 1, 1)
                        WAIT(sync, "c_wcvt", prevh)
                    d = sync.dma_start(
                        out=AR[0][:, :].rearrange("p (t c) -> p t c", t=NDT // 2),
                        in_=src.rearrange("(t p) c -> p t c", p=128)[
                            :, 8 * h2:8 * (h2 + 1), 128 * idx:128 * (idx + 1)])
                    INC(d, "d_w", ("qkv", ci, h2), 16)
            for hf in range(NHF):
                for ct in range(NDT):
                    if hf == 0 and ct == 0:
                        WTOT(sync, "c_ev")
                    for h2 in range(2):
                        prevh = (("p", hf, ct, 0) if h2 == 1 else
                                 (("p", hf, ct - 1, 1) if ct > 0 else
                                  (("p", hf - 1, NDT - 1, 1) if hf > 0 else ("qkv", NCI - 1, 1))))
                        WAIT(sync, "c_wcvt", prevh)
                        d = sync.dma_start(
                            out=AR[0][:, :].rearrange("p (t c) -> p t c", t=NDT // 2),
                            in_=wp.rearrange("(t p) c -> p t c", p=128)[
                                :, 8 * h2:8 * (h2 + 1), 128 * ct:128 * (ct + 1)])
                        INC(d, "d_w", ("p", hf, ct, h2), 16)
                    if (hf, ct) != (0, 0):
                        pc = (hf, ct - 1) if ct > 0 else (hf - 1, NDT - 1)
                        WAIT(sync, "c_rb", pc)
                        WAIT(sync, "d_r1w", pc)
                    if ct == 0 and hf >= 1:
                        WAIT(sync, "c_mrs", hf - 1)
                    d = sync.dma_start(out=AR[1][:, 0:1024],
                                       in_=xT[128 * ct:128 * (ct + 1), 1024 * hf:1024 * (hf + 1)])
                    INC(d, "d_xr", (hf, ct), 16)
            for tc in range(NTC):
                for ot in range(NDT):
                    if ot >= 1:
                        WAIT(sync, "c_rb2", (tc, ot - 1))
                        WAIT(sync, "d_r2w", (tc, ot - 1))
                    elif tc == 0:
                        WAIT(sync, "c_mrs", 1)
                        WAIT(sync, "c_rb", (1, NDT - 1))
                        WAIT(sync, "d_r1w", (1, NDT - 1))
                    else:
                        WAIT(sync, "c_mr2s", tc - 1)
                        WAIT(sync, "c_rb2", (tc - 1, NDT - 1))
                        WAIT(sync, "d_r2w", (tc - 1, NDT - 1))
                    WAIT(sync, "d_x2w", (tc, ot))
                    d = sync.dma_start(out=AR[1][:, 0:1024],
                                       in_=x2s[128 * ot:128 * (ot + 1), 1024 * tc:1024 * (tc + 1)])
                    INC(d, "d_x2r", (tc, ot), 16)
            WTOT(sync, "d_out")
            if DEBUG_DUMP:
                dsem = S["d_cst"]
                base = L.tot["d_cst"]
                pairs = [(dmp_r1, r1s), (dmp_x2, x2s), (dmp_r2, r2s),
                         (dmp_g, ccgu_out[:, 0:ISH]), (dmp_d, ccd_out), (dmp_din, ccd_in)]
                for k2, (dst2, src2) in enumerate(pairs):
                    sync.dma_start(out=dst2[:, :], in_=src2[:, :]).then_inc(dsem, 16)
                sync.wait_ge(dsem, base + 16 * len(pairs))

        @block.scalar
        def _(scalar):
            CURENG[0] = "scalar"
            scalar.wait_ge(S["d_cst"], L.tot["d_cst"])
            for i in range(NDT):
                WAIT(scalar, "d_x", i)
                if i % 2 == 0:
                    c = scalar.copy(out=XB[i][:, :], in_=AR[1][:, :])
                else:
                    scalar.copy(out=XB[i][:, 0:1024], in_=QY14F[:, :])
                    c = scalar.copy(out=XB[i][:, 1024:2048], in_=QY15F[:, :])
                INC(c, "c_xbf", i)
            def wcvt_qkv(ci):
                for h2 in range(2):
                    WAIT(scalar, "d_w", ("qkv", ci, h2))
                    if ci >= 2 and h2 == 0:
                        WAIT(scalar, "c_mmqk", (ci - 2, NHF - 1))
                    c = scalar.copy(out=WBF[ci % 2][:, 1024 * h2:1024 * (h2 + 1)],
                                    in_=AR[0][:, :])
                    INC(c, "c_wcvt", ("qkv", ci, h2))

            wcvt_qkv(0)
            for ci, (kind, idx) in enumerate(QKV):
                if ci + 1 < NCI:
                    wcvt_qkv(ci + 1)
                for hf in range(NHF):
                    gi = ci * NHF + hf
                    b0 = qkv_bank(gi)
                    WAIT(scalar, "c_mmqk", (ci, hf))
                    if kind in ("q", "k"):
                        g2 = QKGROUPS.index((ci, hf))
                        if g2 >= 1:
                            WAIT(scalar, "c_rope", QKGROUPS[g2 - 1])
                        scalar.copy(out=AR[1][:, 0:1024], in_=BK(b0, 1024))
                        scalar.copy(out=AR[1][0:64, 1024:2048], in_=AR[1][64:128, 0:1024])
                        c = scalar.copy(out=AR[1][64:128, 1024:2048], in_=AR[1][0:64, 0:1024])
                        INC(c, "c_evq", (ci, hf))
                    else:
                        vgi = (ci - (NCI - NKV)) * NHF + hf
                        if vgi >= 1:
                            pci = NCI - NKV + (vgi - 1) // NHF
                            WAIT(scalar, "c_mmqk", ("vt", pci, (vgi - 1) % NHF))
                        c = scalar.copy(out=VSTG, in_=BK(b0, 1024))
                        INC(c, "c_evq", (ci, hf))
                        WAIT(scalar, "c_mmqk", ("vt", ci, hf))
                        for b in range(8):
                            tt = hf * 8 + b
                            c = scalar.copy(out=VB[tt][:, 128 * idx:128 * (idx + 1)],
                                            in_=AUX[:, 128 * b:128 * (b + 1)])
                        INC(c, "c_vn", (ci, hf))
            for ii, (h, qc, kt) in enumerate(INST):
                qlo, qn, rel, diag = inst_geom(h, qc, kt)
                hqi = HQIDX[(h, qc)]
                sl = ii % 2
                WAIT(scalar, "c_sc", ii)
                if ii >= 2:
                    WAIT(scalar, "c_pv", ii - 2)
                c = scalar.activation(out=PTv[sl][:, 0:qn], in_=BK(ii % 3)[:, 0:qn],
                                      func=AF.Exp, bias=-EXPC, scale=SCALE)
                INC(c, "c_ex", ii)
                if kt == lastkt(qc):
                    WAIT(scalar, "c_pv", ii)
                    if hqi >= 2:
                        WAIT(scalar, "c_ev", hqi - 2)
                    scalar.activation(out=MB[:, 0:512], in_=BK(3 + hqi % 2), func=AF.Ln)
                    c = scalar.activation(out=RCv[hqi % 2], in_=MB[:, 0:512],
                                          func=AF.Exp, scale=-1.0)
                    INC(c, "c_rcp", hqi)

            def wcvt_proj(hf, ct):
                gpi = NCI + hf * NDT + ct
                for h2 in range(2):
                    WAIT(scalar, "d_w", ("p", hf, ct, h2))
                    if h2 == 0:
                        prev2 = gpi - 2
                        if prev2 >= NCI:
                            hf2, ct2 = divmod(prev2 - NCI, NDT)
                            WAIT(scalar, "c_mmp", (hf2, ct2))
                        else:
                            WAIT(scalar, "c_mmqk", (prev2, NHF - 1))
                    c = scalar.copy(out=WBF[gpi % 2][:, 1024 * h2:1024 * (h2 + 1)],
                                    in_=AR[0][:, :])
                    INC(c, "c_wcvt", ("p", hf, ct, h2))

            def proj_ct(hf, ct):
                nxt = (hf, ct + 1) if ct + 1 < NDT else ((hf + 1, 0) if hf + 1 < NHF else None)
                if nxt is not None:
                    wcvt_proj(*nxt)
                WAIT(scalar, "c_rv", (hf, ct, 1))
                if (hf, ct) != (0, 0):
                    pc = (hf, ct - 1) if ct > 0 else (hf - 1, NDT - 1)
                    WAIT(scalar, "c_mst", pc)
                if hf == 1 and ct == 0:
                    WAIT(scalar, "c_mrv", 0)
                rt = AR[1][:, 0:1024]
                scalar.copy(out=RBv, in_=rt)
                c = scalar.activation(out=RSQv, in_=RBv, func=AF.Square)
                INC(c, "c_rb", (hf, ct))
                d = scalar.dma_start(out=r1s[128 * ct:128 * (ct + 1), 1024 * hf:1024 * (hf + 1)],
                                     in_=rt)
                INC(d, "d_r1w", (hf, ct), 16)

            def norm1_ct(hf, ct):
                if ct >= 1:
                    WAIT(scalar, "d_x2w", (hf, ct - 1))
                    WAIT(scalar, "c_nv", (hf, ct - 1))
                WAIT(scalar, "d_r1w", (hf, ct))
                if ct == 0:
                    WAIT(scalar, "c_mrv", hf)
                d = scalar.dma_start(out=AR[1][:, 1024:2048],
                                     in_=r1s[128 * ct:128 * (ct + 1), 1024 * hf:1024 * (hf + 1)])
                INC(d, "d_r1r", (hf, ct), 16)
                WAIT(scalar, "c_nv", (hf, ct))
                c = scalar.copy(out=XB[ct][:, 1024 * hf:1024 * (hf + 1)],
                                in_=AR[1][:, 1024:2048])
                INC(c, "c_x2b", (hf, ct))
                d = scalar.dma_start(out=x2s[128 * ct:128 * (ct + 1), 1024 * hf:1024 * (hf + 1)],
                                     in_=AR[1][:, 1024:2048])
                INC(d, "d_x2w", (hf, ct), 16)

            wcvt_proj(0, 0)
            for ct in range(NDT):
                proj_ct(0, ct)
            WAIT(scalar, "c_mrv", 0)
            c = scalar.activation(out=RSB[:, 0:1024], in_=AR[1][:, 0:1024], func=AF.Sqrt)
            INC(c, "c_mrs", 0)
            for ct in range(NDT):
                proj_ct(1, ct)
                norm1_ct(0, ct)
            WAIT(scalar, "c_mrv", 1)
            c = scalar.activation(out=RSB[:, 1024:2048], in_=AR[1][:, 0:1024], func=AF.Sqrt)
            INC(c, "c_mrs", 1)

            def sil_it(tc, it):
                WAIT(scalar, "c_mmg", (tc, it, "g"))
                w = gu_banks(tc * NIT + it)
                scalar.activation(out=hpiece(it, 0), in_=BK(w[0]), func=AF.Silu)
                c = scalar.activation(out=hpiece(it, 1), in_=BK(w[1]), func=AF.Silu)
                INC(c, "c_sil", (tc, it))

            def down_ot(tc, ot):
                WAIT(scalar, "c_rv2", (tc, ot, 1))
                if tc == 0 and ot == 0:
                    WAIT(scalar, "c_mmp", (1, NDT - 1))
                else:
                    p1 = (tc, ot - 1) if ot >= 1 else (tc - 1, NDT - 1)
                    WAIT(scalar, "c_mst2", p1)
                if tc == 1 and ot == 0:
                    WAIT(scalar, "c_mr2v", 0)
                r2t = AR[1][:, 0:1024]
                scalar.copy(out=RB2v, in_=r2t)
                c = scalar.activation(out=RSQ2v, in_=RB2v, func=AF.Square)
                INC(c, "c_rb2", (tc, ot))
                d = scalar.dma_start(out=r2s[128 * ot:128 * (ot + 1), 1024 * tc:1024 * (tc + 1)],
                                     in_=r2t)
                INC(d, "d_r2w", (tc, ot), 16)

            def out_dma(tc, ot):
                WAIT(scalar, "c_nv2", (tc, ot))
                d = scalar.dma_start(
                    out=outT[128 * ot:128 * (ot + 1), 1024 * tc:1024 * (tc + 1)],
                    in_=AR[1][:, 1024:2048])
                INC(d, "d_out", (tc, ot), 16)

            def norm2_dma(tc, ot):
                if ot >= 1:
                    out_dma(tc, ot - 1)
                    WAIT(scalar, "d_out", (tc, ot - 1))
                else:
                    WAIT(scalar, "c_mr2s", tc)
                    if tc == 0:
                        WAIT(scalar, "d_x2w", (1, NDT - 1))
                        WAIT(scalar, "c_nv", (1, NDT - 1))
                    else:
                        WAIT(scalar, "d_out", (0, NDT - 1))
                WAIT(scalar, "d_r2w", (tc, ot))
                d = scalar.dma_start(out=AR[1][:, 1024:2048],
                                     in_=r2s[128 * ot:128 * (ot + 1), 1024 * tc:1024 * (tc + 1)])
                INC(d, "d_r2r", (tc, ot), 16)

            for it in range(NIT):
                sil_it(0, it)
                if it < NDT:
                    norm1_ct(1, it)
            for ot in range(NDT):
                down_ot(0, ot)
            WAIT(scalar, "c_mr2v", 0)
            c = scalar.activation(out=RSB[:, 0:1024], in_=AR[1][:, 0:1024], func=AF.Sqrt)
            INC(c, "c_mr2s", 0)
            for it in range(NIT):
                sil_it(1, it)
                if it < NDT:
                    norm2_dma(0, it)
                elif it == NDT:
                    out_dma(0, NDT - 1)
            for ot in range(NDT):
                down_ot(1, ot)
            WAIT(scalar, "c_mr2v", 1)
            c = scalar.activation(out=RSB[:, 1024:2048], in_=AR[1][:, 0:1024], func=AF.Sqrt)
            INC(c, "c_mr2s", 1)
            for ot in range(NDT):
                norm2_dma(1, ot)
            out_dma(1, NDT - 1)

        @block.tensor
        def _(tensor):
            CURENG[0] = "tensor"
            for ci, (kind, idx) in enumerate(QKV):
                for hf in range(NHF):
                    gi = ci * NHF + hf
                    b0 = qkv_bank(gi)
                    if gi == 0:
                        WAIT(tensor, "c_xbf", NDT - 1)
                    WAIT(tensor, "c_wcvt", ("qkv", ci, 1))
                    if gi >= 2:
                        WAIT(tensor, "c_evq", ((gi - 2) // NHF, (gi - 2) % NHF))
                    last = None
                    for dt in range(NDT):
                        for c in range(2):
                            last = tensor.matmul(
                                BK(b0 + c),
                                WBF[ci % 2][:, 128 * dt:128 * (dt + 1)],
                                XB[dt][:, 1024 * hf + 512 * c:1024 * hf + 512 * (c + 1)],
                                start=(dt == 0), stop=(dt == NDT - 1))
                    INC(last, "c_mmqk", (ci, hf))
                    if kind == "v":
                        WAIT(tensor, "c_evq", (ci, hf))
                        vgi = (ci - (NCI - NKV)) * NHF + hf
                        if vgi >= 1:
                            pci = NCI - NKV + (vgi - 1) // NHF
                            WAIT(tensor, "c_vn", (pci, (vgi - 1) % NHF))
                        last = None
                        for b in range(8):
                            last = tensor.transpose(AUX[:, 128 * b:128 * (b + 1)],
                                                    VSTG[:, 128 * b:128 * (b + 1)],
                                                    IDENT)
                        INC(last, "c_mmqk", ("vt", ci, hf))

            def emit_scores(ii):
                h, qc, kt = INST[ii]
                qlo, qn, rel, diag = inst_geom(h, qc, kt)
                kv = h // REP
                if ii == 0:
                    WTOT(tensor, "c_rope")
                    WTOT(tensor, "c_vn")
                    WTOT(tensor, "c_evq")
                if ii >= 3:
                    WAIT(tensor, "c_ex", ii - 3)
                mmi = tensor.matmul(BK(ii % 3)[:, 0:qn],
                                    KTB[kv][:, 128 * kt:128 * (kt + 1)],
                                    QY[h][:, qlo:qlo + qn], start=True, stop=True)
                INC(mmi, "c_sc", ii)

            def emit_sumspv(ii):
                h, qc, kt = INST[ii]
                qlo, qn, rel, diag = inst_geom(h, qc, kt)
                kv = h // REP
                hqi = HQIDX[(h, qc)]
                sl = ii % 2
                lk = lastkt(qc)
                if diag:
                    WAIT(tensor, "c_dg", ii)
                else:
                    WAIT(tensor, "c_ex", ii)
                if kt == 0 and hqi >= 2:
                    WAIT(tensor, "c_rcp", hqi - 2)
                    WAIT(tensor, "c_ev", hqi - 2)
                tensor.matmul(BK(3 + hqi % 2)[:, rel:rel + qn], ONES, PTv[sl][:, 0:qn],
                              start=(kt == 0), stop=(kt == lk))
                mmi = tensor.matmul(BK(5 + hqi % 2)[:, rel:rel + qn],
                                    VB[kt][:, 128 * kv:128 * (kv + 1)], PTv[sl][:, 0:qn],
                                    start=(kt == 0), stop=(kt == lk))
                INC(mmi, "c_pv", ii)

            prev = None
            for ii in range(NINST):
                emit_scores(ii)
                if prev is not None:
                    emit_sumspv(prev)
                prev = ii
            emit_sumspv(prev)

            for hf in range(NHF):
                for ct in range(NDT):
                    i = hf * NDT + ct
                    c0, c1 = pool3(i)
                    if i == 0:
                        WTOT(tensor, "c_ex")
                        WTOT(tensor, "c_rcp")
                        WTOT(tensor, "c_ev")
                        WTOT(tensor, "c_deq")
                    WAIT(tensor, "c_wcvt", ("p", hf, ct, 1))
                    if i >= 2:
                        hf2, ct2 = divmod(i - 2, NDT)
                        WAIT(tensor, "c_rv", (hf2, ct2, 1))
                    if i >= 1:
                        hf1, ct1 = divmod(i - 1, NDT)
                        WAIT(tensor, "c_rv", (hf1, ct1, 0))
                    gpi = NCI + i
                    last = None
                    for dt in range(NDT):
                        last = tensor.matmul(BK(c0), WBF[gpi % 2][:, 128 * dt:128 * (dt + 1)],
                                             QY[dt][:, 1024 * hf:1024 * hf + 512],
                                             start=(dt == 0), stop=(dt == NDT - 1))
                        last = tensor.matmul(BK(c1), WBF[gpi % 2][:, 128 * dt:128 * (dt + 1)],
                                             QY[dt][:, 1024 * hf + 512:1024 * (hf + 1)],
                                             start=(dt == 0), stop=(dt == NDT - 1))
                    INC(last, "c_mmp", (hf, ct))
                    WAIT(tensor, "c_rb", (hf, ct))
                    if hf == 1 and ct == 0:
                        WAIT(tensor, "c_mrv", 0)
                    tensor.matmul(BK(3), OON, RBv[:, 0:512],
                                  start=(ct == 0), stop=(ct == NDT - 1))
                    tensor.matmul(BK(4), OON, RBv[:, 512:1024],
                                  start=(ct == 0), stop=(ct == NDT - 1))
                    tensor.matmul(BK(5), OON, RSQv[:, 0:512],
                                  start=(ct == 0), stop=(ct == NDT - 1))
                    last = tensor.matmul(BK(6), OON, RSQv[:, 512:1024],
                                         start=(ct == 0), stop=(ct == NDT - 1))
                    INC(last, "c_mst", (hf, ct))

            for tc in range(NTC):
                for it in range(NIT):
                    g2 = tc * NIT + it
                    w = gu_banks(g2)
                    if g2 == 0:
                        WAIT(tensor, "c_x2b", (0, NDT - 1))
                        WAIT(tensor, "c_mrv", 1)
                        WAIT(tensor, "c_rv", (1, NDT - 1, 1))
                    if tc == 1 and it == 0:
                        WAIT(tensor, "c_x2b", (1, NDT - 1))
                        WAIT(tensor, "c_mr2v", 0)
                        WAIT(tensor, "c_rv2", (0, NDT - 1, 1))
                    if g2 >= 2:
                        tc2, it2 = divmod(g2 - 2, NIT)
                        WAIT(tensor, "c_h", (tc2, it2))
                    for mi, m in enumerate(("g", "u")):
                        WAIT(tensor, "d_gu", (tc, it, m))
                        if m == "u" and g2 >= 1:
                            tc1, it1 = divmod(g2 - 1, NIT)
                            WAIT(tensor, "c_sil", (tc1, it1))
                        fbk = FBIDX[("gu", tc, it, m)]
                        last = None
                        for dt in range(NDT):
                            for c in range(2):
                                last = tensor.matmul(
                                    BK(w[2 * mi + c]),
                                    FBv[fbk % 2][:, 128 * dt:128 * (dt + 1)],
                                    XB[dt][:, 1024 * tc + 512 * c:1024 * tc + 512 * (c + 1)],
                                    start=(dt == 0), stop=(dt == NDT - 1))
                        INC(last, "c_mmg", (tc, it, m))
                for ot in range(NDT):
                    j = tc * NDT + ot
                    c0, c1 = pool3(j)
                    if ot == 0:
                        WAIT(tensor, "c_h", (tc, NIT - 1))
                        WAIT(tensor, "c_sil", (tc, NIT - 1))
                    if j >= 2:
                        tc2, ot2 = divmod(j - 2, NDT)
                        WAIT(tensor, "c_rv2", (tc2, ot2, 1))
                    if j >= 1:
                        tc1, ot1 = divmod(j - 1, NDT)
                        WAIT(tensor, "c_rv2", (tc1, ot1, 0))
                    for gi, (g0, gn) in enumerate(DGRP):
                        WAIT(tensor, "d_wd", (tc, ot, gi))
                        fbk = FBIDX[("wd", tc, ot, gi)]
                        last = None
                        for itl in range(gn):
                            it = g0 + itl
                            last = tensor.matmul(BK(c0), FBv[fbk % 2][:, 128 * itl:128 * (itl + 1)],
                                                 hpiece(it, 0),
                                                 start=(it == 0), stop=(it == NIT - 1))
                            last = tensor.matmul(BK(c1), FBv[fbk % 2][:, 128 * itl:128 * (itl + 1)],
                                                 hpiece(it, 1),
                                                 start=(it == 0), stop=(it == NIT - 1))
                        INC(last, "c_mmd", (tc, ot, gi))
                    WAIT(tensor, "c_rb2", (tc, ot))
                    if tc == 1 and ot == 0:
                        WAIT(tensor, "c_mr2v", 0)
                    tensor.matmul(BK(3), OON, RB2v[:, 0:512],
                                  start=(ot == 0), stop=(ot == NDT - 1))
                    tensor.matmul(BK(4), OON, RB2v[:, 512:1024],
                                  start=(ot == 0), stop=(ot == NDT - 1))
                    tensor.matmul(BK(5), OON, RSQ2v[:, 0:512],
                                  start=(ot == 0), stop=(ot == NDT - 1))
                    last = tensor.matmul(BK(6), OON, RSQ2v[:, 512:1024],
                                         start=(ot == 0), stop=(ot == NDT - 1))
                    INC(last, "c_mst2", (tc, ot))

        @block.vector
        def _(vector):
            CURENG[0] = "vector"
            st = {"deq": 0}

            def deq_tile():
                mi = st["deq"]
                if mi >= NDQ:
                    return
                m, dt = DQ[mi]
                WAIT(vector, "c_idxc", (m, dt))
                if mi >= 1:
                    WAIT(vector, "d_wdeq", DQ[mi - 1])
                if m == "d":
                    rn = _d_rows[dt[0]][1]
                    cn = _d_cols[dt[1]][1]
                else:
                    rn, cn = 128, ISH
                acc = WDQ[mi % 2]
                v = None
                for j in range(LUTN):
                    dst = acc if j == 0 else DQT
                    v2 = vector.tensor_scalar(out=dst[0:rn, 0:cn], in0=IDXBF[mi % 2][0:rn, 0:cn],
                                              scalar1=float(j), scalar2=LUT[0:rn, j:j + 1],
                                              op0=ALU.is_equal, op1=ALU.mult)
                    if j == 0:
                        v = v2
                    else:
                        v = vector.tensor_add(out=acc[0:rn, 0:cn], in0=acc[0:rn, 0:cn],
                                              in1=DQT[0:rn, 0:cn])
                INC(v, "c_deq", (m, dt))
                st["deq"] += 1

            for g2, (ci, hf) in enumerate(QKGROUPS):
                kind, idx = QKV[ci]
                WAIT(vector, "c_evq", (ci, hf))
                tgt = QY[idx] if kind == "q" else KTB[idx]
                vector.tensor_mul(out=AR[1][:, 0:1024], in0=AR[1][:, 0:1024],
                                  in1=MB[:, 1024 * hf:1024 * (hf + 1)])
                vector.tensor_mul(out=AR[1][:, 1024:2048], in0=AR[1][:, 1024:2048],
                                  in1=RSB[:, 1024 * hf:1024 * (hf + 1)])
                v = vector.tensor_add(out=tgt[:, 1024 * hf:1024 * (hf + 1)],
                                      in0=AR[1][:, 0:1024], in1=AR[1][:, 1024:2048])
                INC(v, "c_rope", (ci, hf))
                deq_tile()
            for ii, (h, qc, kt) in enumerate(INST):
                qlo, qn, rel, diag = inst_geom(h, qc, kt)
                hqi = HQIDX[(h, qc)]
                if diag:
                    WAIT(vector, "c_ex", ii)
                    v = vector.tensor_mul(out=PTv[ii % 2][:, 0:128], in0=PTv[ii % 2][:, 0:128],
                                          in1=MASK01)
                    INC(v, "c_dg", ii)
                if kt == lastkt(qc):
                    deq_tile()
                    WAIT(vector, "c_rcp", hqi)
                    v = vector.tensor_mul(out=QY[h][:, QC * qc:QC * (qc + 1)],
                                          in0=BK(5 + hqi % 2), in1=RCv[hqi % 2])
                    INC(v, "c_ev", hqi)
            while st["deq"] < NDQ:
                deq_tile()

            def rv_ct(hf, ct):
                i = hf * NDT + ct
                c0, c1 = pool3(i)
                WAIT(vector, "c_mmp", (hf, ct))
                WAIT(vector, "d_xr", (hf, ct))
                rt = AR[1][:, 0:1024]
                v = vector.tensor_add(out=rt[:, 0:512], in0=rt[:, 0:512], in1=BK(c0))
                INC(v, "c_rv", (hf, ct, 0))
                v = vector.tensor_add(out=rt[:, 512:1024], in0=rt[:, 512:1024], in1=BK(c1))
                INC(v, "c_rv", (hf, ct, 1))

            def mchain(cols, s1b, s2b, t1m, dwsem, dwkey, sem, key, pre=()):
                for (ps, pk) in pre:
                    WAIT(vector, ps, pk)
                vector.tensor_copy(out=MB[:, cols], in_=BK(s1b, 1024))
                vector.tensor_mul(out=t1m, in0=MB[:, cols], in1=MB[:, cols])
                vector.scalar_tensor_tensor(out=AR[1][:, 1024:2048], in0=BK(s2b, 1024),
                                            scalar=EPS, in1=t1m,
                                            op0=ALU.add, op1=ALU.subtract)
                WAIT(vector, dwsem, dwkey)
                v = vector.reciprocal(out=AR[1][:, 0:1024], in_=AR[1][:, 1024:2048])
                INC(v, sem, key)

            def nv_ct(hf, ct):
                WAIT(vector, "d_r1r", (hf, ct))
                if ct == 0:
                    WAIT(vector, "c_mrs", hf)
                slt = AR[1][:, 1024:2048]
                vector.tensor_sub(out=slt, in0=slt, in1=MB[:, 1024 * hf:1024 * (hf + 1)])
                v = vector.tensor_mul(out=slt, in0=slt, in1=RSB[:, 1024 * hf:1024 * (hf + 1)])
                INC(v, "c_nv", (hf, ct))

            for ct in range(NDT):
                rv_ct(0, ct)
            WAIT(vector, "c_mst", (0, NDT - 1))
            WTOT(vector, "d_wdeq")
            mchain(slice(0, 1024), 3, 5, T1M, "d_r1w", (0, NDT - 1), "c_mrv", 0)
            for ct in range(NDT):
                rv_ct(1, ct)
                nv_ct(0, ct)
            WAIT(vector, "c_mst", (1, NDT - 1))
            mchain(slice(1024, 2048), 3, 5, T1M, "d_r1w", (1, NDT - 1), "c_mrv", 1,
                   pre=[("c_x2b", (0, NDT - 1)), ("d_x2w", (0, NDT - 1))])

            def hmul(tc, it):
                g2 = tc * NIT + it
                w = gu_banks(g2)
                WAIT(vector, "c_sil", (tc, it))
                WAIT(vector, "c_mmg", (tc, it, "u"))
                if tc == 1 and it == 0:
                    WAIT(vector, "c_mmd", (0, NDT - 1, 2))
                vector.tensor_mul(out=hpiece(it, 0), in0=hpiece(it, 0), in1=BK(w[2]))
                v = vector.tensor_mul(out=hpiece(it, 1), in0=hpiece(it, 1), in1=BK(w[3]))
                INC(v, "c_h", (tc, it))

            def rv2_ot(tc, ot):
                j = tc * NDT + ot
                c0, c1 = pool3(j)
                WAIT(vector, "c_mmd", (tc, ot, 2))
                WAIT(vector, "d_x2r", (tc, ot))
                r2t = AR[1][:, 0:1024]
                v = vector.tensor_add(out=r2t[:, 0:512], in0=r2t[:, 0:512], in1=BK(c0))
                INC(v, "c_rv2", (tc, ot, 0))
                v = vector.tensor_add(out=r2t[:, 512:1024], in0=r2t[:, 512:1024], in1=BK(c1))
                INC(v, "c_rv2", (tc, ot, 1))

            def nv2_ot(tc, ot):
                WAIT(vector, "d_r2r", (tc, ot))
                if ot == 0:
                    WAIT(vector, "c_mr2s", tc)
                slt = AR[1][:, 1024:2048]
                vector.tensor_sub(out=slt, in0=slt, in1=MB[:, 1024 * tc:1024 * (tc + 1)])
                v = vector.tensor_mul(out=slt, in0=slt, in1=RSB[:, 1024 * tc:1024 * (tc + 1)])
                INC(v, "c_nv2", (tc, ot))

            for it in range(NIT):
                hmul(0, it)
                if it < NDT:
                    nv_ct(1, it)
            for ot in range(NDT):
                rv2_ot(0, ot)
            WAIT(vector, "c_mst2", (0, NDT - 1))
            WAIT(vector, "c_mmp", (1, NDT - 1))
            mchain(slice(0, 1024), 3, 5, T1M2, "d_r2w", (0, NDT - 1), "c_mr2v", 0)
            for it in range(NIT):
                hmul(1, it)
                if it < NDT:
                    nv2_ot(0, it)
            for ot in range(NDT):
                rv2_ot(1, ot)
            WAIT(vector, "c_mst2", (1, NDT - 1))
            mchain(slice(1024, 2048), 3, 5, T1M2, "d_r2w", (1, NDT - 1), "c_mr2v", 1,
                   pre=[("d_out", (0, NDT - 1))])
            for ot in range(NDT):
                nv2_ot(1, ot)

        @block.gpsimd
        def _(gpsimd):
            CURENG[0] = "gpsimd"
            st = {"conv": 0, "dma": 0, "out": 0}

            def idx_dma():
                mi = st["dma"]
                m, dt = DQ[mi]
                if mi >= 1:
                    WAIT(gpsimd, "c_idxc", DQ[mi - 1])
                if m == "d":
                    ri, cj = dt
                    r0, rn = _d_rows[ri]
                    c0, cn = _d_cols[cj]
                    d = gpsimd.dma_start(out=IDX32[0:rn, 0:cn], in_=didx[r0:r0 + rn, c0:c0 + cn])
                else:
                    srct = gidx if m == "g" else uidx
                    d = gpsimd.dma_start(out=IDX32[:, :], in_=srct[128 * dt:128 * (dt + 1), :])
                INC(d, "d_idx", (m, dt), 16)
                st["dma"] += 1

            def idx_conv():
                mi = st["conv"]
                m, dt = DQ[mi]
                WAIT(gpsimd, "d_idx", (m, dt))
                if mi >= 2:
                    WAIT(gpsimd, "c_deq", DQ[mi - 2])
                if m == "d":
                    rn = _d_rows[dt[0]][1]
                    cn = _d_cols[dt[1]][1]
                    c = gpsimd.tensor_copy(out=IDXBF[mi % 2][0:rn, 0:cn], in_=IDX32[0:rn, 0:cn])
                else:
                    c = gpsimd.tensor_copy(out=IDXBF[mi % 2][:, :], in_=IDX32[:, :])
                INC(c, "c_idxc", (m, dt))
                st["conv"] += 1

            def wdeq_dma():
                mi = st["out"]
                m, dt = DQ[mi]
                WAIT(gpsimd, "c_deq", (m, dt))
                if m == "d":
                    ri, cj = dt
                    r0, rn = _d_rows[ri]
                    c0, cn = _d_cols[cj]
                    d = gpsimd.dma_start(out=ccd_in[r0:r0 + rn, c0:c0 + cn],
                                         in_=WDQ[mi % 2][0:rn, 0:cn])
                else:
                    co = 0 if m == "g" else ISH
                    d = gpsimd.dma_start(out=ccgu_in[128 * dt:128 * (dt + 1), co:co + ISH],
                                         in_=WDQ[mi % 2][:, :])
                INC(d, "d_wdeq", (m, dt), 16)
                st["out"] += 1

            def deq_step():
                if st["dma"] == 0:
                    idx_dma()
                if st["conv"] < NDQ:
                    idx_conv()
                if st["dma"] < NDQ:
                    idx_dma()
                if st["out"] < st["conv"] - 1:
                    wdeq_dma()

            while st["conv"] < 38:
                deq_step()
            for ii, (h, qc, kt) in enumerate(INST):
                qlo, qn, rel, diag = inst_geom(h, qc, kt)
                if kt == lastkt(qc) and (st["conv"] < NDQ or st["out"] < st["conv"] - 1):
                    deq_step()
            while st["conv"] < NDQ:
                deq_step()
            while st["out"] < NDQ:
                wdeq_dma()
            WAIT(gpsimd, "d_wdeq", DQ[2 * NDT - 1])
            cc = gpsimd.collective_compute(
                "AllGather", ALU.bypass, replica_groups=[list(range(NCORES))],
                ins=[ccgu_in[:, :]], outs=[ccgu_out[:, :]])
            INC(cc, "d_ccg", 0, 1)
            WAIT(gpsimd, "d_wdeq", DQ[NDQ - 1])
            cc = gpsimd.collective_compute(
                "AllGather", ALU.bypass, replica_groups=[list(range(NCORES))],
                ins=[ccd_in[:, :]], outs=[ccd_out[:, :]])
            INC(cc, "d_ccd", 0, 1)
            for tc in range(NTC):
                for it in range(NIT):
                    for m in ("g", "u"):
                        fbk = FBIDX[("gu", tc, it, m)]
                        if fbk >= 2:
                            cs, ck = fb_consume(FBSEQ[fbk - 2])
                            WAIT(gpsimd, cs, ck)
                        if tc == 0 and it == 0:
                            gpsimd.wait_ge(S["d_ccg"], 1)
                            if m == "g":
                                WTOT(gpsimd, "c_pv")
                                WTOT(gpsimd, "c_ev")
                                WTOT(gpsimd, "d_wdeq")
                        off = 0 if m == "g" else ISH
                        srcr = ccgu_out.rearrange("(r t p) c -> r p t c", r=NCORES, p=128)
                        dstr = FBv[fbk % 2].rearrange("p (t c) -> p t c", t=NDT)
                        parts = gu_split(it)
                        for pi2, (blk, c0, c1, doff) in enumerate(parts):
                            d = gpsimd.dma_start(out=dstr[:, :, doff:doff + (c1 - c0)],
                                                 in_=srcr[blk][:, :, off + c0:off + c1])
                            if pi2 < len(parts) - 1:
                                RAWINC(d, "d_gu", 16)
                            else:
                                INCF(d, "d_gu", (tc, it, m), 16)
                for ot in range(NDT):
                    for gi, (g0, gn) in enumerate(DGRP):
                        fbk = FBIDX[("wd", tc, ot, gi)]
                        cs, ck = fb_consume(FBSEQ[fbk - 2])
                        WAIT(gpsimd, cs, ck)
                        if tc == 0 and ot == 0 and gi == 0:
                            gpsimd.wait_ge(S["d_ccd"], 1)
                        srcr = ccd_out.rearrange("(t p) c -> p t c", p=128)
                        d = gpsimd.dma_start(
                            out=FBv[fbk % 2][:, 0:128 * gn].rearrange("p (t c) -> p t c", t=gn),
                            in_=srcr[:, g0:g0 + gn, 128 * ot:128 * (ot + 1)])
                        INC(d, "d_wd", (tc, ot, gi), 16)

    for s in SEMS:
        assert emitted[s] == L.tot[s], (s, emitted[s], L.tot[s])
    return nc


def _rope_tables():
    dd = np.arange(0, HD, 2, dtype=np.float64) / HD
    inv = 1.0 / np.power(10000.0, dd)
    t = np.arange(T, dtype=np.float64)
    fr = inv[:, None] * t[None, :]
    cosT = np.concatenate([np.cos(fr), np.cos(fr)], axis=0).astype(np.float32)
    sinT = np.concatenate([-np.sin(fr), np.sin(fr)], axis=0).astype(np.float32)
    return np.ascontiguousarray(cosT), np.ascontiguousarray(sinT)


def _consts():
    kl = np.arange(128)[:, None]
    ql = np.arange(128)[None, :]
    mask01 = (ql >= kl).astype(ml_dtypes.bfloat16)
    ones = np.ones((128, 128), dtype=ml_dtypes.bfloat16)
    ident = np.eye(128, dtype=ml_dtypes.bfloat16)
    oon = np.full((128, 128), 1.0 / D, dtype=ml_dtypes.bfloat16)
    return np.ascontiguousarray(np.concatenate([mask01, ones, ident, oon], axis=1))


_NC_CACHE = None
TRACE = None
DEBUG_DUMP = False


def kernel(L0_attn_x, L0_attn_Wq, L0_attn_Wk, L0_attn_Wv, L0_attn_Wproj,
           L0_ffn_lut, L0_ffn_gate_idx, L0_ffn_up_idx, L0_ffn_down_idx):
    global _NC_CACHE
    x = np.asarray(L0_attn_x, dtype=np.float32)
    wqh = np.ascontiguousarray(np.asarray(L0_attn_Wq, dtype=np.float32))
    wkh = np.ascontiguousarray(np.asarray(L0_attn_Wk, dtype=np.float32))
    wvh = np.ascontiguousarray(np.asarray(L0_attn_Wv, dtype=np.float32))
    wph = np.ascontiguousarray(np.asarray(L0_attn_Wproj, dtype=np.float32))
    lut = np.asarray(L0_ffn_lut, dtype=np.float32)
    gi = np.asarray(L0_ffn_gate_idx, dtype=np.int32)
    ui = np.asarray(L0_ffn_up_idx, dtype=np.int32)
    di = np.asarray(L0_ffn_down_idx, dtype=np.int32)

    lutr = np.ascontiguousarray(np.broadcast_to(lut[None, :], (128, LUTN)))
    cosT, sinT = _rope_tables()
    cst = _consts()
    giT = np.ascontiguousarray(gi.T)
    uiT = np.ascontiguousarray(ui.T)
    diT = np.ascontiguousarray(di.T)

    if _NC_CACHE is None:
        _NC_CACHE = build_nc()
    nc = _NC_CACHE

    in_maps = []
    for b in range(NCORES):
        in_maps.append({
            "xT": np.ascontiguousarray(x[b].T),
            "wq": wqh, "wk": wkh, "wv": wvh, "wp": wph,
            "lutr": lutr,
            "gidx": np.ascontiguousarray(giT[:, ISH * b:ISH * (b + 1)]),
            "uidx": np.ascontiguousarray(uiT[:, ISH * b:ISH * (b + 1)]),
            "didx": np.ascontiguousarray(diT[ISH * b:ISH * (b + 1), :]),
            "cosd": cosT, "sind": sinT, "cstd": cst,
            "actc": np.ascontiguousarray(np.broadcast_to(
                np.array([-EXPC, SCALE, -1.0, 0.0], dtype=np.float32)[None, :], (128, 4))),
        })
    res = run_bass_kernel_spmd(nc, in_maps, core_ids=list(range(NCORES)))
    out = np.empty((NCORES, T, D), dtype=np.float32)
    for b in range(NCORES):
        out[b] = res.results[b]["outT"].T
    return out


# version 21
# speedup vs baseline: 1.0234x; 1.0053x over previous
import math
import contextlib
import numpy as np
import ml_dtypes
import concourse.bass as bass
import concourse.mybir as mybir
from concourse.bass_utils import run_bass_kernel_spmd

F32 = mybir.dt.float32
BF16 = mybir.dt.bfloat16
I32 = mybir.dt.int32
AF = mybir.ActivationFunctionType
ALU = mybir.AluOpType

NCORES = 8
D = 2048
T = 2048
NH = 16
NKV = 4
HD = 128
REP = NH // NKV
I = 5504
NDT = D // 128
NTT = T // 128
NIT = I // 128
ISH = I // NCORES
QC = 512
NQC = T // QC
NHF = 2
NTC = 2
LUTN = 16
EXPC = 6.0
SCALE = 1.0 / math.sqrt(HD)
EPS = 1e-5

QKV = [("k", i) for i in range(NKV)] + [("q", h) for h in range(NH)] + [("v", i) for i in range(NKV)]
NCI = len(QKV)
QKGROUPS = [(ci, hf) for ci in range(NCI) for hf in range(NHF) if QKV[ci][0] in ("q", "k")]

HQ = [(h, qc) for h in range(NH) for qc in range(NQC)]
HQIDX = {v: i for i, v in enumerate(HQ)}
INST = []
for (h, qc) in HQ:
    for kt in range((qc + 1) * QC // 128):
        INST.append((h, qc, kt))
NINST = len(INST)


def lastkt(qc):
    return (qc + 1) * QC // 128 - 1


def inst_geom(h, qc, kt):
    qlo = max(QC * qc, 128 * kt)
    qn = QC * (qc + 1) - qlo
    rel = qlo - QC * qc
    diag = 128 * kt >= QC * qc
    return qlo, qn, rel, diag


_d_rows = []
_r = 0
while _r < ISH:
    if ISH - _r < 128:
        _r = ISH - 128
    _d_rows.append((_r, 128))
    _r += 128
_d_cols = []
_c = 0
while _c < D:
    _d_cols.append((_c, min(ISH, D - _c)))
    _c += min(ISH, D - _c)
DQ = ([("g", t) for t in range(NDT)] + [("u", t) for t in range(NDT)]
      + [("d", (ri, ci)) for ri in range(len(_d_rows)) for ci in range(len(_d_cols))])
NDQ = len(DQ)


def gu_split(it):
    lo, hi = 128 * it, 128 * (it + 1)
    out = []
    for r in range(lo // ISH, (hi - 1) // ISH + 1):
        a, b = max(lo, ISH * r), min(hi, ISH * (r + 1))
        out.append((r, a - ISH * r, b - ISH * r, a - lo))
    return out


DGRP = [(0, 16), (16, 16), (32, 11)]

FBSEQ = []
for _tc in range(NTC):
    for _it in range(NIT):
        for _m in ("g", "u"):
            FBSEQ.append(("gu", _tc, _it, _m))
    for _ot in range(NDT):
        for _gi in range(3):
            FBSEQ.append(("wd", _tc, _ot, _gi))
FBIDX = {v: i for i, v in enumerate(FBSEQ)}


def fb_consume(entry):
    if entry[0] == "gu":
        return ("c_mmg", (entry[1], entry[2], entry[3]))
    return ("c_mmd", (entry[1], entry[2], entry[3]))


def pool3(i):
    return ((2 * i) % 3, (2 * i + 1) % 3)


def gu_banks(g):
    s = (4 * g) % 7
    return [(s + k) % 7 for k in range(4)]


class Ledger:
    def __init__(self):
        self.tot = {}
        self.ev = {}

    def inc(self, sem, key, n=1):
        self.tot[sem] = self.tot.get(sem, 0) + n
        assert (sem, key) not in self.ev, (sem, key)
        self.ev[(sem, key)] = self.tot[sem]

    def at(self, sem, key):
        return self.ev[(sem, key)]


def plan():
    L = Ledger()
    for i in range(5):
        L.inc("d_cst", i, 16)
    for i in range(NDT):
        L.inc("d_x", i, 16 if i % 2 == 0 else 32)
    for ci in range(NCI):
        for h2 in range(2):
            L.inc("d_w", ("qkv", ci, h2), 16)
    for hf in range(NHF):
        for ct in range(NDT):
            for h2 in range(2):
                L.inc("d_w", ("p", hf, ct, h2), 16)
            L.inc("d_xr", (hf, ct), 16)
    for tc in range(NTC):
        for it in range(NIT):
            for m in ("g", "u"):
                L.inc("d_gu", (tc, it, m), 16 * len(gu_split(it)))
        for ot in range(NDT):
            for gi in range(3):
                L.inc("d_wd", (tc, ot, gi), 16)
            L.inc("d_x2r", (tc, ot), 16)
        for ot in range(NDT):
            L.inc("d_out", (tc, ot), 16)
    for key in DQ:
        L.inc("d_idx", key, 16)
        L.inc("d_wdeq", key, 16)
    for hf in range(NHF):
        for ct in range(NDT):
            L.inc("d_r1w", (hf, ct), 16)
        for ct in range(NDT):
            L.inc("d_r1r", (hf, ct), 16)
            L.inc("d_x2w", (hf, ct), 16)
    for tc in range(NTC):
        for ot in range(NDT):
            L.inc("d_r2w", (tc, ot), 16)
        for ot in range(NDT):
            L.inc("d_r2r", (tc, ot), 16)
    L.inc("d_ccg", 0, 1)
    L.inc("d_ccd", 0, 1)
    for ii, (h, qc, kt) in enumerate(INST):
        if inst_geom(h, qc, kt)[3]:
            L.inc("c_dg", ii)
    for key in DQ:
        L.inc("c_idxc", key)
    for i in range(NDT):
        L.inc("c_xbf", i)
    for ci in range(NCI):
        for h2 in range(2):
            L.inc("c_wcvt", ("qkv", ci, h2))
        for hf in range(NHF):
            L.inc("c_evq", (ci, hf))
            if QKV[ci][0] == "v":
                L.inc("c_vn", (ci, hf))
    for ii in range(NINST):
        L.inc("c_ex", ii)
    for hqi in range(len(HQ)):
        L.inc("c_rcp", hqi)
    for hf in range(NHF):
        for ct in range(NDT):
            for h2 in range(2):
                L.inc("c_wcvt", ("p", hf, ct, h2))
            L.inc("c_rb", (hf, ct))
        L.inc("c_mrs", hf)
        for ct in range(NDT):
            L.inc("c_x2b", (hf, ct))
    for tc in range(NTC):
        for it in range(NIT):
            L.inc("c_sil", (tc, it))
        for ot in range(NDT):
            L.inc("c_rb2", (tc, ot))
        L.inc("c_mr2s", tc)
    for ci in range(NCI):
        for hf in range(NHF):
            L.inc("c_mmqk", (ci, hf))
            if QKV[ci][0] == "v":
                L.inc("c_mmqk", ("vt", ci, hf))
    for ii in range(NINST):
        L.inc("c_sc", ii)
        L.inc("c_pv", ii)
    for hf in range(NHF):
        for ct in range(NDT):
            L.inc("c_mmp", (hf, ct))
            L.inc("c_mst", (hf, ct))
    for tc in range(NTC):
        for it in range(NIT):
            for m in ("g", "u"):
                L.inc("c_mmg", (tc, it, m))
        for ot in range(NDT):
            for gi in range(3):
                L.inc("c_mmd", (tc, ot, gi))
            L.inc("c_mst2", (tc, ot))
    for g in QKGROUPS:
        L.inc("c_rope", g)
    for key in DQ:
        L.inc("c_deq", key)
    for hqi in range(len(HQ)):
        L.inc("c_ev", hqi)
    for hf in range(NHF):
        for ct in range(NDT):
            L.inc("c_rv", (hf, ct, 0))
            L.inc("c_rv", (hf, ct, 1))
        L.inc("c_mrv", hf)
        for ct in range(NDT):
            L.inc("c_nv", (hf, ct))
    for tc in range(NTC):
        for it in range(NIT):
            L.inc("c_h", (tc, it))
        for ot in range(NDT):
            L.inc("c_rv2", (tc, ot, 0))
            L.inc("c_rv2", (tc, ot, 1))
        L.inc("c_mr2v", tc)
        for ot in range(NDT):
            L.inc("c_nv2", (tc, ot))
    return L


SEMS = ["d_cst", "d_x", "d_w", "d_xr", "d_gu", "d_wd", "d_x2r", "d_out",
        "d_idx", "d_wdeq", "d_r1w", "d_r1r", "d_x2w", "d_r2w", "d_r2r",
        "d_ccg", "d_ccd",
        "c_idxc", "c_xbf", "c_wcvt", "c_evq", "c_vn", "c_ex", "c_rcp", "c_rb",
        "c_mrs", "c_x2b", "c_sil", "c_rb2", "c_mr2s",
        "c_mmqk", "c_sc", "c_pv", "c_mmp", "c_mst", "c_mmg", "c_mmd", "c_mst2",
        "c_rope", "c_deq", "c_dg", "c_ev", "c_rv", "c_mrv", "c_nv",
        "c_h", "c_rv2", "c_mr2v", "c_nv2"]


def build_nc():
    L = plan()
    emitted = {s: 0 for s in SEMS}

    nc = bass.Bass("TRN2", target_bir_lowering=False, debug=False,
                   dynamic_dma_scratch_size=16384)

    xT = nc.dram_tensor("xT", [D, T], F32, kind="ExternalInput").ap()
    wq = nc.dram_tensor("wq", [128, NDT * D], F32, kind="ExternalInput").ap()
    wk = nc.dram_tensor("wk", [128, NKV * D], F32, kind="ExternalInput").ap()
    wv = nc.dram_tensor("wv", [128, NKV * D], F32, kind="ExternalInput").ap()
    wp = nc.dram_tensor("wp", [128, NDT * D], F32, kind="ExternalInput").ap()
    lutr = nc.dram_tensor("lutr", [128, LUTN], F32, kind="ExternalInput").ap()
    gidx = nc.dram_tensor("gidx", [D, ISH], I32, kind="ExternalInput").ap()
    uidx = nc.dram_tensor("uidx", [D, ISH], I32, kind="ExternalInput").ap()
    didx = nc.dram_tensor("didx", [ISH, D], I32, kind="ExternalInput").ap()
    cosd = nc.dram_tensor("cosd", [128, T], F32, kind="ExternalInput").ap()
    sind = nc.dram_tensor("sind", [128, T], F32, kind="ExternalInput").ap()
    cstd = nc.dram_tensor("cstd", [128, 4 * 128], BF16, kind="ExternalInput").ap()
    actc = nc.dram_tensor("actc", [128, 4], F32, kind="ExternalInput").ap()
    outT = nc.dram_tensor("outT", [D, T], F32, kind="ExternalOutput").ap()

    ccgu_in = nc.dram_tensor("ccgu_in", [D, 2 * ISH], BF16, kind="Internal").ap()
    ccd_in = nc.dram_tensor("ccd_in", [ISH, D], BF16, kind="Internal").ap()
    ccgu_out = nc.dram_tensor("ccgu_out", [NCORES * D, 2 * ISH], BF16, kind="Internal").ap()
    ccd_out = nc.dram_tensor("ccd_out", [I, D], BF16, kind="Internal").ap()
    r1s = nc.dram_tensor("r1s", [D, T], F32, kind="Internal").ap()
    x2s = nc.dram_tensor("x2s", [D, T], F32, kind="Internal").ap()
    r2s = nc.dram_tensor("r2s", [D, T], F32, kind="Internal").ap()
    if DEBUG_DUMP:
        dmp_r1 = nc.dram_tensor("dmp_r1", [D, T], F32, kind="ExternalOutput").ap()
        dmp_x2 = nc.dram_tensor("dmp_x2", [D, T], F32, kind="ExternalOutput").ap()
        dmp_r2 = nc.dram_tensor("dmp_r2", [D, T], F32, kind="ExternalOutput").ap()
        dmp_g = nc.dram_tensor("dmp_g", [NCORES * D, ISH], BF16, kind="ExternalOutput").ap()
        dmp_d = nc.dram_tensor("dmp_d", [I, D], BF16, kind="ExternalOutput").ap()
        dmp_din = nc.dram_tensor("dmp_din", [ISH, D], BF16, kind="ExternalOutput").ap()

    es = contextlib.ExitStack()
    with es:
        sbt = lambda name, shape, dt: es.enter_context(nc.sbuf_tensor(name, shape, dt))
        XB = [sbt(f"XB{i}", [128, T], BF16) for i in range(NDT)]
        QY = [sbt(f"QY{i}", [128, T], BF16) for i in range(NH)]
        KTB = [sbt(f"KT{i}", [128, T], BF16) for i in range(NKV)]
        VB = [sbt(f"V{i}", [128, NKV * HD], BF16) for i in range(NTT)]
        MB = sbt("MB", [128, T], F32)
        RSB = sbt("RSB", [128, T], F32)
        AR = [sbt("AR0", [128, 1024], F32), sbt("AR1", [128, T], F32)]
        WBF = [sbt(f"WBF{i}", [128, D], BF16) for i in range(2)]
        POOL = sbt("POOL", [128, 4096], BF16)
        IDX32 = sbt("IDX32", [128, ISH], I32)
        LUT = sbt("LUT", [128, LUTN], F32)
        CST = sbt("CST", [128, 4 * 128], BF16)
        ACTC = sbt("ACTC", [128, 4], F32)
        nc.const_aps.register(F32, -EXPC, ACTC[:, 0:1])
        nc.const_aps.register(F32, SCALE, ACTC[:, 1:2])
        nc.const_aps.register(F32, -1.0, ACTC[:, 2:3])
        MAIN = es.enter_context(nc.psum_tensor("MAIN", [128, 3584], F32))
        AUX = es.enter_context(nc.psum_tensor("AUX", [128, 1024], BF16))

        MASK01 = CST[:, 0:128]
        ONES = CST[:, 128:256]
        IDENT = CST[:, 256:384]
        OON = CST[:, 384:512]

        PTv = [POOL[:, 0:512], POOL[:, 512:1024]]
        VSTG = POOL[:, 0:1024]
        IDXBF = [POOL[:, 1024:1024 + ISH], POOL[:, 3088:3088 + ISH]]
        DQT = POOL[:, 1712:1712 + ISH]
        WDQ = [POOL[:, 2400:2400 + ISH], POOL[:, 2400:2400 + ISH]]
        FBv = [POOL[:, 0:2048], POOL[:, 2048:4096]]
        RBv = KTB[0][:, 0:1024]
        RSQv = KTB[0][:, 1024:2048]
        T1M = KTB[1][:, 0:1024]
        RB2v = WBF[0][:, 0:1024]
        RSQ2v = WBF[0][:, 1024:2048]
        T1M2 = WBF[1][:, 0:1024]
        RCv = [AR[0][:, 0:512], AR[0][:, 512:1024], None]
        QY14F = QY[14].bitcast(F32)
        QY15F = QY[15].bitcast(F32)

        S = {s: es.enter_context(nc.semaphore(s)) for s in SEMS}
        block = es.enter_context(nc.Block())

        def BK(b, cols=512):
            return MAIN[:, 512 * b:512 * b + cols]

        CURENG = ["?"]

        def WAIT(eng, s, key):
            if TRACE is not None:
                TRACE.append((CURENG[0], "wait", s, L.at(s, key)))
            eng.wait_ge(S[s], L.at(s, key))

        def WTOT(eng, s):
            if TRACE is not None:
                TRACE.append((CURENG[0], "wait", s, L.tot[s]))
            eng.wait_ge(S[s], L.tot[s])

        def INC(instr, s, key, n=1):
            if TRACE is not None:
                TRACE.append((CURENG[0], "inc", s, n))
            emitted[s] += n
            assert L.ev[(s, key)] == emitted[s], (s, key, L.ev[(s, key)], emitted[s])
            instr.then_inc(S[s], n)

        def RAWINC(instr, s, n=16):
            if TRACE is not None:
                TRACE.append((CURENG[0], "inc", s, n))
            emitted[s] += n
            instr.then_inc(S[s], n)

        def INCF(instr, s, key, n=16):
            if TRACE is not None:
                TRACE.append((CURENG[0], "inc", s, n))
            emitted[s] += n
            assert L.ev[(s, key)] == emitted[s], (s, key, L.ev[(s, key)], emitted[s])
            instr.then_inc(S[s], n)

        PIECES = []
        for j in range(NH):
            for c in range(4):
                PIECES.append((QY[j], 512 * c))
        for j in range(NKV):
            for c in range(4):
                PIECES.append((KTB[j], 512 * c))
        for j in range(NTT):
            PIECES.append((VB[j], 0))

        def hpiece(it, c):
            t, off = PIECES[2 * it + c]
            return t[:, off:off + 512]

        def qkv_bank(gi):
            return 0 if gi % 2 == 0 else 2

        @block.sync
        def _(sync):
            CURENG[0] = "sync"
            INC(sync.dma_start(out=LUT[:, :], in_=lutr[:, :]), "d_cst", 0, 16)
            INC(sync.dma_start(out=CST[:, :], in_=cstd[:, :]), "d_cst", 1, 16)
            INC(sync.dma_start(out=MB[:, :], in_=cosd[:, :]), "d_cst", 2, 16)
            INC(sync.dma_start(out=RSB[:, :], in_=sind[:, :]), "d_cst", 3, 16)
            INC(sync.dma_start(out=ACTC[:, :], in_=actc[:, :]), "d_cst", 4, 16)
            for i in range(NDT):
                if i >= 2:
                    WAIT(sync, "c_xbf", i - 2)
                if i % 2 == 0:
                    d = sync.dma_start(out=AR[1][:, :], in_=xT[128 * i:128 * (i + 1), :])
                    INC(d, "d_x", i, 16)
                else:
                    d = sync.dma_start(out=QY14F[:, :],
                                       in_=xT[128 * i:128 * (i + 1), 0:1024])
                    RAWINC(d, "d_x", 16)
                    d = sync.dma_start(out=QY15F[:, :],
                                       in_=xT[128 * i:128 * (i + 1), 1024:2048])
                    INCF(d, "d_x", i, 16)
            for ci, (kind, idx) in enumerate(QKV):
                src = {"q": wq, "k": wk, "v": wv}[kind]
                for h2 in range(2):
                    if (ci, h2) != (0, 0):
                        prevh = ("qkv", ci, h2 - 1) if h2 == 1 else ("qkv", ci - 1, 1)
                        WAIT(sync, "c_wcvt", prevh)
                    d = sync.dma_start(
                        out=AR[0][:, :],
                        in_=src[:, 2048 * idx + 1024 * h2:2048 * idx + 1024 * (h2 + 1)])
                    INC(d, "d_w", ("qkv", ci, h2), 16)
            for hf in range(NHF):
                for ct in range(NDT):
                    if hf == 0 and ct == 0:
                        WTOT(sync, "c_ev")
                    for h2 in range(2):
                        prevh = (("p", hf, ct, 0) if h2 == 1 else
                                 (("p", hf, ct - 1, 1) if ct > 0 else
                                  (("p", hf - 1, NDT - 1, 1) if hf > 0 else ("qkv", NCI - 1, 1))))
                        WAIT(sync, "c_wcvt", prevh)
                        d = sync.dma_start(
                            out=AR[0][:, :],
                            in_=wp[:, 2048 * ct + 1024 * h2:2048 * ct + 1024 * (h2 + 1)])
                        INC(d, "d_w", ("p", hf, ct, h2), 16)
                    if (hf, ct) != (0, 0):
                        pc = (hf, ct - 1) if ct > 0 else (hf - 1, NDT - 1)
                        WAIT(sync, "c_rb", pc)
                        WAIT(sync, "d_r1w", pc)
                    if ct == 0 and hf >= 1:
                        WAIT(sync, "c_mrs", hf - 1)
                    d = sync.dma_start(out=AR[1][:, 0:1024],
                                       in_=xT[128 * ct:128 * (ct + 1), 1024 * hf:1024 * (hf + 1)])
                    INC(d, "d_xr", (hf, ct), 16)
            for tc in range(NTC):
                for ot in range(NDT):
                    if ot >= 1:
                        WAIT(sync, "c_rb2", (tc, ot - 1))
                        WAIT(sync, "d_r2w", (tc, ot - 1))
                    elif tc == 0:
                        WAIT(sync, "c_mrs", 1)
                        WAIT(sync, "c_rb", (1, NDT - 1))
                        WAIT(sync, "d_r1w", (1, NDT - 1))
                    else:
                        WAIT(sync, "c_mr2s", tc - 1)
                        WAIT(sync, "c_rb2", (tc - 1, NDT - 1))
                        WAIT(sync, "d_r2w", (tc - 1, NDT - 1))
                    WAIT(sync, "d_x2w", (tc, ot))
                    d = sync.dma_start(out=AR[1][:, 0:1024],
                                       in_=x2s[128 * ot:128 * (ot + 1), 1024 * tc:1024 * (tc + 1)])
                    INC(d, "d_x2r", (tc, ot), 16)
            WTOT(sync, "d_out")
            if DEBUG_DUMP:
                dsem = S["d_cst"]
                base = L.tot["d_cst"]
                pairs = [(dmp_r1, r1s), (dmp_x2, x2s), (dmp_r2, r2s),
                         (dmp_g, ccgu_out[:, 0:ISH]), (dmp_d, ccd_out), (dmp_din, ccd_in)]
                for k2, (dst2, src2) in enumerate(pairs):
                    sync.dma_start(out=dst2[:, :], in_=src2[:, :]).then_inc(dsem, 16)
                sync.wait_ge(dsem, base + 16 * len(pairs))

        @block.scalar
        def _(scalar):
            CURENG[0] = "scalar"
            scalar.wait_ge(S["d_cst"], L.tot["d_cst"])
            for i in range(NDT):
                WAIT(scalar, "d_x", i)
                if i % 2 == 0:
                    c = scalar.copy(out=XB[i][:, :], in_=AR[1][:, :])
                else:
                    scalar.copy(out=XB[i][:, 0:1024], in_=QY14F[:, :])
                    c = scalar.copy(out=XB[i][:, 1024:2048], in_=QY15F[:, :])
                INC(c, "c_xbf", i)
            def wcvt_qkv(ci):
                for h2 in range(2):
                    WAIT(scalar, "d_w", ("qkv", ci, h2))
                    if ci >= 2 and h2 == 0:
                        WAIT(scalar, "c_mmqk", (ci - 2, NHF - 1))
                    c = scalar.copy(out=WBF[ci % 2][:, 1024 * h2:1024 * (h2 + 1)],
                                    in_=AR[0][:, :])
                    INC(c, "c_wcvt", ("qkv", ci, h2))

            wcvt_qkv(0)
            for ci, (kind, idx) in enumerate(QKV):
                if ci + 1 < NCI:
                    wcvt_qkv(ci + 1)
                for hf in range(NHF):
                    gi = ci * NHF + hf
                    b0 = qkv_bank(gi)
                    WAIT(scalar, "c_mmqk", (ci, hf))
                    if kind in ("q", "k"):
                        g2 = QKGROUPS.index((ci, hf))
                        if g2 >= 1:
                            WAIT(scalar, "c_rope", QKGROUPS[g2 - 1])
                        scalar.copy(out=AR[1][:, 0:1024], in_=BK(b0, 1024))
                        scalar.copy(out=AR[1][0:64, 1024:2048], in_=AR[1][64:128, 0:1024])
                        c = scalar.copy(out=AR[1][64:128, 1024:2048], in_=AR[1][0:64, 0:1024])
                        INC(c, "c_evq", (ci, hf))
                    else:
                        vgi = (ci - (NCI - NKV)) * NHF + hf
                        if vgi >= 1:
                            pci = NCI - NKV + (vgi - 1) // NHF
                            WAIT(scalar, "c_mmqk", ("vt", pci, (vgi - 1) % NHF))
                        c = scalar.copy(out=VSTG, in_=BK(b0, 1024))
                        INC(c, "c_evq", (ci, hf))
                        WAIT(scalar, "c_mmqk", ("vt", ci, hf))
                        for b in range(8):
                            tt = hf * 8 + b
                            c = scalar.copy(out=VB[tt][:, 128 * idx:128 * (idx + 1)],
                                            in_=AUX[:, 128 * b:128 * (b + 1)])
                        INC(c, "c_vn", (ci, hf))
            for ii, (h, qc, kt) in enumerate(INST):
                qlo, qn, rel, diag = inst_geom(h, qc, kt)
                hqi = HQIDX[(h, qc)]
                sl = ii % 2
                WAIT(scalar, "c_sc", ii)
                if ii >= 2:
                    WAIT(scalar, "c_pv", ii - 2)
                c = scalar.activation(out=PTv[sl][:, 0:qn], in_=BK(ii % 3)[:, 0:qn],
                                      func=AF.Exp, bias=-EXPC, scale=SCALE)
                INC(c, "c_ex", ii)
                if kt == lastkt(qc):
                    WAIT(scalar, "c_pv", ii)
                    if hqi >= 2:
                        WAIT(scalar, "c_ev", hqi - 2)
                    scalar.activation(out=MB[:, 0:512], in_=BK(3 + hqi % 2), func=AF.Ln)
                    c = scalar.activation(out=RCv[hqi % 2], in_=MB[:, 0:512],
                                          func=AF.Exp, scale=-1.0)
                    INC(c, "c_rcp", hqi)

            def wcvt_proj(hf, ct):
                gpi = NCI + hf * NDT + ct
                for h2 in range(2):
                    WAIT(scalar, "d_w", ("p", hf, ct, h2))
                    if h2 == 0:
                        prev2 = gpi - 2
                        if prev2 >= NCI:
                            hf2, ct2 = divmod(prev2 - NCI, NDT)
                            WAIT(scalar, "c_mmp", (hf2, ct2))
                        else:
                            WAIT(scalar, "c_mmqk", (prev2, NHF - 1))
                    c = scalar.copy(out=WBF[gpi % 2][:, 1024 * h2:1024 * (h2 + 1)],
                                    in_=AR[0][:, :])
                    INC(c, "c_wcvt", ("p", hf, ct, h2))

            def proj_ct(hf, ct):
                nxt = (hf, ct + 1) if ct + 1 < NDT else ((hf + 1, 0) if hf + 1 < NHF else None)
                if nxt is not None:
                    wcvt_proj(*nxt)
                WAIT(scalar, "c_rv", (hf, ct, 1))
                if (hf, ct) != (0, 0):
                    pc = (hf, ct - 1) if ct > 0 else (hf - 1, NDT - 1)
                    WAIT(scalar, "c_mst", pc)
                if hf == 1 and ct == 0:
                    WAIT(scalar, "c_mrv", 0)
                rt = AR[1][:, 0:1024]
                scalar.copy(out=RBv, in_=rt)
                c = scalar.activation(out=RSQv, in_=RBv, func=AF.Square)
                INC(c, "c_rb", (hf, ct))
                d = scalar.dma_start(out=r1s[128 * ct:128 * (ct + 1), 1024 * hf:1024 * (hf + 1)],
                                     in_=rt)
                INC(d, "d_r1w", (hf, ct), 16)

            def norm1_ct(hf, ct):
                if ct >= 1:
                    WAIT(scalar, "d_x2w", (hf, ct - 1))
                    WAIT(scalar, "c_nv", (hf, ct - 1))
                WAIT(scalar, "d_r1w", (hf, ct))
                if ct == 0:
                    WAIT(scalar, "c_mrv", hf)
                d = scalar.dma_start(out=AR[1][:, 1024:2048],
                                     in_=r1s[128 * ct:128 * (ct + 1), 1024 * hf:1024 * (hf + 1)])
                INC(d, "d_r1r", (hf, ct), 16)
                WAIT(scalar, "c_nv", (hf, ct))
                c = scalar.copy(out=XB[ct][:, 1024 * hf:1024 * (hf + 1)],
                                in_=AR[1][:, 1024:2048])
                INC(c, "c_x2b", (hf, ct))
                d = scalar.dma_start(out=x2s[128 * ct:128 * (ct + 1), 1024 * hf:1024 * (hf + 1)],
                                     in_=AR[1][:, 1024:2048])
                INC(d, "d_x2w", (hf, ct), 16)

            wcvt_proj(0, 0)
            for ct in range(NDT):
                proj_ct(0, ct)
            WAIT(scalar, "c_mrv", 0)
            c = scalar.activation(out=RSB[:, 0:1024], in_=AR[1][:, 0:1024], func=AF.Sqrt)
            INC(c, "c_mrs", 0)
            for ct in range(NDT):
                proj_ct(1, ct)
                norm1_ct(0, ct)
            WAIT(scalar, "c_mrv", 1)
            c = scalar.activation(out=RSB[:, 1024:2048], in_=AR[1][:, 0:1024], func=AF.Sqrt)
            INC(c, "c_mrs", 1)

            def sil_it(tc, it):
                WAIT(scalar, "c_mmg", (tc, it, "g"))
                w = gu_banks(tc * NIT + it)
                scalar.activation(out=hpiece(it, 0), in_=BK(w[0]), func=AF.Silu)
                c = scalar.activation(out=hpiece(it, 1), in_=BK(w[1]), func=AF.Silu)
                INC(c, "c_sil", (tc, it))

            def down_ot(tc, ot):
                WAIT(scalar, "c_rv2", (tc, ot, 1))
                if tc == 0 and ot == 0:
                    WAIT(scalar, "c_mmp", (1, NDT - 1))
                else:
                    p1 = (tc, ot - 1) if ot >= 1 else (tc - 1, NDT - 1)
                    WAIT(scalar, "c_mst2", p1)
                if tc == 1 and ot == 0:
                    WAIT(scalar, "c_mr2v", 0)
                r2t = AR[1][:, 0:1024]
                scalar.copy(out=RB2v, in_=r2t)
                c = scalar.activation(out=RSQ2v, in_=RB2v, func=AF.Square)
                INC(c, "c_rb2", (tc, ot))
                d = scalar.dma_start(out=r2s[128 * ot:128 * (ot + 1), 1024 * tc:1024 * (tc + 1)],
                                     in_=r2t)
                INC(d, "d_r2w", (tc, ot), 16)

            def out_dma(tc, ot):
                WAIT(scalar, "c_nv2", (tc, ot))
                d = scalar.dma_start(
                    out=outT[128 * ot:128 * (ot + 1), 1024 * tc:1024 * (tc + 1)],
                    in_=AR[1][:, 1024:2048])
                INC(d, "d_out", (tc, ot), 16)

            def norm2_dma(tc, ot):
                if ot >= 1:
                    out_dma(tc, ot - 1)
                    WAIT(scalar, "d_out", (tc, ot - 1))
                else:
                    WAIT(scalar, "c_mr2s", tc)
                    if tc == 0:
                        WAIT(scalar, "d_x2w", (1, NDT - 1))
                        WAIT(scalar, "c_nv", (1, NDT - 1))
                    else:
                        WAIT(scalar, "d_out", (0, NDT - 1))
                WAIT(scalar, "d_r2w", (tc, ot))
                d = scalar.dma_start(out=AR[1][:, 1024:2048],
                                     in_=r2s[128 * ot:128 * (ot + 1), 1024 * tc:1024 * (tc + 1)])
                INC(d, "d_r2r", (tc, ot), 16)

            for it in range(NIT):
                sil_it(0, it)
                if it < NDT:
                    norm1_ct(1, it)
            for ot in range(NDT):
                down_ot(0, ot)
            WAIT(scalar, "c_mr2v", 0)
            c = scalar.activation(out=RSB[:, 0:1024], in_=AR[1][:, 0:1024], func=AF.Sqrt)
            INC(c, "c_mr2s", 0)
            for it in range(NIT):
                sil_it(1, it)
                if it < NDT:
                    norm2_dma(0, it)
                elif it == NDT:
                    out_dma(0, NDT - 1)
            for ot in range(NDT):
                down_ot(1, ot)
            WAIT(scalar, "c_mr2v", 1)
            c = scalar.activation(out=RSB[:, 1024:2048], in_=AR[1][:, 0:1024], func=AF.Sqrt)
            INC(c, "c_mr2s", 1)
            for ot in range(NDT):
                norm2_dma(1, ot)
            out_dma(1, NDT - 1)

        @block.tensor
        def _(tensor):
            CURENG[0] = "tensor"
            for ci, (kind, idx) in enumerate(QKV):
                for hf in range(NHF):
                    gi = ci * NHF + hf
                    b0 = qkv_bank(gi)
                    if gi == 0:
                        WAIT(tensor, "c_xbf", NDT - 1)
                    WAIT(tensor, "c_wcvt", ("qkv", ci, 1))
                    if gi >= 2:
                        WAIT(tensor, "c_evq", ((gi - 2) // NHF, (gi - 2) % NHF))
                    last = None
                    for dt in range(NDT):
                        for c in range(2):
                            last = tensor.matmul(
                                BK(b0 + c),
                                WBF[ci % 2][:, 128 * dt:128 * (dt + 1)],
                                XB[dt][:, 1024 * hf + 512 * c:1024 * hf + 512 * (c + 1)],
                                start=(dt == 0), stop=(dt == NDT - 1))
                    INC(last, "c_mmqk", (ci, hf))
                    if kind == "v":
                        WAIT(tensor, "c_evq", (ci, hf))
                        vgi = (ci - (NCI - NKV)) * NHF + hf
                        if vgi >= 1:
                            pci = NCI - NKV + (vgi - 1) // NHF
                            WAIT(tensor, "c_vn", (pci, (vgi - 1) % NHF))
                        last = None
                        for b in range(8):
                            last = tensor.transpose(AUX[:, 128 * b:128 * (b + 1)],
                                                    VSTG[:, 128 * b:128 * (b + 1)],
                                                    IDENT)
                        INC(last, "c_mmqk", ("vt", ci, hf))

            def emit_scores(ii):
                h, qc, kt = INST[ii]
                qlo, qn, rel, diag = inst_geom(h, qc, kt)
                kv = h // REP
                if ii == 0:
                    WTOT(tensor, "c_rope")
                    WTOT(tensor, "c_vn")
                    WTOT(tensor, "c_evq")
                if ii >= 3:
                    WAIT(tensor, "c_ex", ii - 3)
                mmi = tensor.matmul(BK(ii % 3)[:, 0:qn],
                                    KTB[kv][:, 128 * kt:128 * (kt + 1)],
                                    QY[h][:, qlo:qlo + qn], start=True, stop=True)
                INC(mmi, "c_sc", ii)

            def emit_sumspv(ii):
                h, qc, kt = INST[ii]
                qlo, qn, rel, diag = inst_geom(h, qc, kt)
                kv = h // REP
                hqi = HQIDX[(h, qc)]
                sl = ii % 2
                lk = lastkt(qc)
                if diag:
                    WAIT(tensor, "c_dg", ii)
                else:
                    WAIT(tensor, "c_ex", ii)
                if kt == 0 and hqi >= 2:
                    WAIT(tensor, "c_rcp", hqi - 2)
                    WAIT(tensor, "c_ev", hqi - 2)
                tensor.matmul(BK(3 + hqi % 2)[:, rel:rel + qn], ONES, PTv[sl][:, 0:qn],
                              start=(kt == 0), stop=(kt == lk))
                mmi = tensor.matmul(BK(5 + hqi % 2)[:, rel:rel + qn],
                                    VB[kt][:, 128 * kv:128 * (kv + 1)], PTv[sl][:, 0:qn],
                                    start=(kt == 0), stop=(kt == lk))
                INC(mmi, "c_pv", ii)

            prev = None
            for ii in range(NINST):
                emit_scores(ii)
                if prev is not None:
                    emit_sumspv(prev)
                prev = ii
            emit_sumspv(prev)

            for hf in range(NHF):
                for ct in range(NDT):
                    i = hf * NDT + ct
                    c0, c1 = pool3(i)
                    if i == 0:
                        WTOT(tensor, "c_ex")
                        WTOT(tensor, "c_rcp")
                        WTOT(tensor, "c_ev")
                        WTOT(tensor, "c_deq")
                    WAIT(tensor, "c_wcvt", ("p", hf, ct, 1))
                    if i >= 2:
                        hf2, ct2 = divmod(i - 2, NDT)
                        WAIT(tensor, "c_rv", (hf2, ct2, 1))
                    if i >= 1:
                        hf1, ct1 = divmod(i - 1, NDT)
                        WAIT(tensor, "c_rv", (hf1, ct1, 0))
                    gpi = NCI + i
                    last = None
                    for dt in range(NDT):
                        last = tensor.matmul(BK(c0), WBF[gpi % 2][:, 128 * dt:128 * (dt + 1)],
                                             QY[dt][:, 1024 * hf:1024 * hf + 512],
                                             start=(dt == 0), stop=(dt == NDT - 1))
                        last = tensor.matmul(BK(c1), WBF[gpi % 2][:, 128 * dt:128 * (dt + 1)],
                                             QY[dt][:, 1024 * hf + 512:1024 * (hf + 1)],
                                             start=(dt == 0), stop=(dt == NDT - 1))
                    INC(last, "c_mmp", (hf, ct))
                    WAIT(tensor, "c_rb", (hf, ct))
                    if hf == 1 and ct == 0:
                        WAIT(tensor, "c_mrv", 0)
                    tensor.matmul(BK(3), OON, RBv[:, 0:512],
                                  start=(ct == 0), stop=(ct == NDT - 1))
                    tensor.matmul(BK(4), OON, RBv[:, 512:1024],
                                  start=(ct == 0), stop=(ct == NDT - 1))
                    tensor.matmul(BK(5), OON, RSQv[:, 0:512],
                                  start=(ct == 0), stop=(ct == NDT - 1))
                    last = tensor.matmul(BK(6), OON, RSQv[:, 512:1024],
                                         start=(ct == 0), stop=(ct == NDT - 1))
                    INC(last, "c_mst", (hf, ct))

            for tc in range(NTC):
                for it in range(NIT):
                    g2 = tc * NIT + it
                    w = gu_banks(g2)
                    if g2 == 0:
                        WAIT(tensor, "c_x2b", (0, NDT - 1))
                        WAIT(tensor, "c_mrv", 1)
                        WAIT(tensor, "c_rv", (1, NDT - 1, 1))
                    if tc == 1 and it == 0:
                        WAIT(tensor, "c_x2b", (1, NDT - 1))
                        WAIT(tensor, "c_mr2v", 0)
                        WAIT(tensor, "c_rv2", (0, NDT - 1, 1))
                    if g2 >= 2:
                        tc2, it2 = divmod(g2 - 2, NIT)
                        WAIT(tensor, "c_h", (tc2, it2))
                    for mi, m in enumerate(("g", "u")):
                        WAIT(tensor, "d_gu", (tc, it, m))
                        if m == "u" and g2 >= 1:
                            tc1, it1 = divmod(g2 - 1, NIT)
                            WAIT(tensor, "c_sil", (tc1, it1))
                        fbk = FBIDX[("gu", tc, it, m)]
                        last = None
                        for dt in range(NDT):
                            for c in range(2):
                                last = tensor.matmul(
                                    BK(w[2 * mi + c]),
                                    FBv[fbk % 2][:, 128 * dt:128 * (dt + 1)],
                                    XB[dt][:, 1024 * tc + 512 * c:1024 * tc + 512 * (c + 1)],
                                    start=(dt == 0), stop=(dt == NDT - 1))
                        INC(last, "c_mmg", (tc, it, m))
                for ot in range(NDT):
                    j = tc * NDT + ot
                    c0, c1 = pool3(j)
                    if ot == 0:
                        WAIT(tensor, "c_h", (tc, NIT - 1))
                        WAIT(tensor, "c_sil", (tc, NIT - 1))
                    if j >= 2:
                        tc2, ot2 = divmod(j - 2, NDT)
                        WAIT(tensor, "c_rv2", (tc2, ot2, 1))
                    if j >= 1:
                        tc1, ot1 = divmod(j - 1, NDT)
                        WAIT(tensor, "c_rv2", (tc1, ot1, 0))
                    for gi, (g0, gn) in enumerate(DGRP):
                        WAIT(tensor, "d_wd", (tc, ot, gi))
                        fbk = FBIDX[("wd", tc, ot, gi)]
                        last = None
                        for itl in range(gn):
                            it = g0 + itl
                            last = tensor.matmul(BK(c0), FBv[fbk % 2][:, 128 * itl:128 * (itl + 1)],
                                                 hpiece(it, 0),
                                                 start=(it == 0), stop=(it == NIT - 1))
                            last = tensor.matmul(BK(c1), FBv[fbk % 2][:, 128 * itl:128 * (itl + 1)],
                                                 hpiece(it, 1),
                                                 start=(it == 0), stop=(it == NIT - 1))
                        INC(last, "c_mmd", (tc, ot, gi))
                    WAIT(tensor, "c_rb2", (tc, ot))
                    if tc == 1 and ot == 0:
                        WAIT(tensor, "c_mr2v", 0)
                    tensor.matmul(BK(3), OON, RB2v[:, 0:512],
                                  start=(ot == 0), stop=(ot == NDT - 1))
                    tensor.matmul(BK(4), OON, RB2v[:, 512:1024],
                                  start=(ot == 0), stop=(ot == NDT - 1))
                    tensor.matmul(BK(5), OON, RSQ2v[:, 0:512],
                                  start=(ot == 0), stop=(ot == NDT - 1))
                    last = tensor.matmul(BK(6), OON, RSQ2v[:, 512:1024],
                                         start=(ot == 0), stop=(ot == NDT - 1))
                    INC(last, "c_mst2", (tc, ot))

        @block.vector
        def _(vector):
            CURENG[0] = "vector"
            st = {"deq": 0}

            def deq_tile():
                mi = st["deq"]
                if mi >= NDQ:
                    return
                m, dt = DQ[mi]
                WAIT(vector, "c_idxc", (m, dt))
                if mi >= 1:
                    WAIT(vector, "d_wdeq", DQ[mi - 1])
                if m == "d":
                    rn = _d_rows[dt[0]][1]
                    cn = _d_cols[dt[1]][1]
                else:
                    rn, cn = 128, ISH
                acc = WDQ[mi % 2]
                v = None
                for j in range(LUTN):
                    dst = acc if j == 0 else DQT
                    v2 = vector.tensor_scalar(out=dst[0:rn, 0:cn], in0=IDXBF[mi % 2][0:rn, 0:cn],
                                              scalar1=float(j), scalar2=LUT[0:rn, j:j + 1],
                                              op0=ALU.is_equal, op1=ALU.mult)
                    if j == 0:
                        v = v2
                    else:
                        v = vector.tensor_add(out=acc[0:rn, 0:cn], in0=acc[0:rn, 0:cn],
                                              in1=DQT[0:rn, 0:cn])
                INC(v, "c_deq", (m, dt))
                st["deq"] += 1

            for g2, (ci, hf) in enumerate(QKGROUPS):
                kind, idx = QKV[ci]
                WAIT(vector, "c_evq", (ci, hf))
                tgt = QY[idx] if kind == "q" else KTB[idx]
                vector.tensor_mul(out=AR[1][:, 0:1024], in0=AR[1][:, 0:1024],
                                  in1=MB[:, 1024 * hf:1024 * (hf + 1)])
                vector.tensor_mul(out=AR[1][:, 1024:2048], in0=AR[1][:, 1024:2048],
                                  in1=RSB[:, 1024 * hf:1024 * (hf + 1)])
                v = vector.tensor_add(out=tgt[:, 1024 * hf:1024 * (hf + 1)],
                                      in0=AR[1][:, 0:1024], in1=AR[1][:, 1024:2048])
                INC(v, "c_rope", (ci, hf))
                deq_tile()
            for ii, (h, qc, kt) in enumerate(INST):
                qlo, qn, rel, diag = inst_geom(h, qc, kt)
                hqi = HQIDX[(h, qc)]
                if diag:
                    WAIT(vector, "c_ex", ii)
                    v = vector.tensor_mul(out=PTv[ii % 2][:, 0:128], in0=PTv[ii % 2][:, 0:128],
                                          in1=MASK01)
                    INC(v, "c_dg", ii)
                if kt == lastkt(qc):
                    deq_tile()
                    WAIT(vector, "c_rcp", hqi)
                    v = vector.tensor_mul(out=QY[h][:, QC * qc:QC * (qc + 1)],
                                          in0=BK(5 + hqi % 2), in1=RCv[hqi % 2])
                    INC(v, "c_ev", hqi)
            while st["deq"] < NDQ:
                deq_tile()

            def rv_ct(hf, ct):
                i = hf * NDT + ct
                c0, c1 = pool3(i)
                WAIT(vector, "c_mmp", (hf, ct))
                WAIT(vector, "d_xr", (hf, ct))
                rt = AR[1][:, 0:1024]
                v = vector.tensor_add(out=rt[:, 0:512], in0=rt[:, 0:512], in1=BK(c0))
                INC(v, "c_rv", (hf, ct, 0))
                v = vector.tensor_add(out=rt[:, 512:1024], in0=rt[:, 512:1024], in1=BK(c1))
                INC(v, "c_rv", (hf, ct, 1))

            def mchain(cols, s1b, s2b, t1m, dwsem, dwkey, sem, key, pre=()):
                for (ps, pk) in pre:
                    WAIT(vector, ps, pk)
                vector.tensor_copy(out=MB[:, cols], in_=BK(s1b, 1024))
                vector.tensor_mul(out=t1m, in0=MB[:, cols], in1=MB[:, cols])
                vector.scalar_tensor_tensor(out=AR[1][:, 1024:2048], in0=BK(s2b, 1024),
                                            scalar=EPS, in1=t1m,
                                            op0=ALU.add, op1=ALU.subtract)
                WAIT(vector, dwsem, dwkey)
                v = vector.reciprocal(out=AR[1][:, 0:1024], in_=AR[1][:, 1024:2048])
                INC(v, sem, key)

            def nv_ct(hf, ct):
                WAIT(vector, "d_r1r", (hf, ct))
                if ct == 0:
                    WAIT(vector, "c_mrs", hf)
                slt = AR[1][:, 1024:2048]
                vector.tensor_sub(out=slt, in0=slt, in1=MB[:, 1024 * hf:1024 * (hf + 1)])
                v = vector.tensor_mul(out=slt, in0=slt, in1=RSB[:, 1024 * hf:1024 * (hf + 1)])
                INC(v, "c_nv", (hf, ct))

            for ct in range(NDT):
                rv_ct(0, ct)
            WAIT(vector, "c_mst", (0, NDT - 1))
            WTOT(vector, "d_wdeq")
            mchain(slice(0, 1024), 3, 5, T1M, "d_r1w", (0, NDT - 1), "c_mrv", 0)
            for ct in range(NDT):
                rv_ct(1, ct)
                nv_ct(0, ct)
            WAIT(vector, "c_mst", (1, NDT - 1))
            mchain(slice(1024, 2048), 3, 5, T1M, "d_r1w", (1, NDT - 1), "c_mrv", 1,
                   pre=[("c_x2b", (0, NDT - 1)), ("d_x2w", (0, NDT - 1))])

            def hmul(tc, it):
                g2 = tc * NIT + it
                w = gu_banks(g2)
                WAIT(vector, "c_sil", (tc, it))
                WAIT(vector, "c_mmg", (tc, it, "u"))
                if tc == 1 and it == 0:
                    WAIT(vector, "c_mmd", (0, NDT - 1, 2))
                vector.tensor_mul(out=hpiece(it, 0), in0=hpiece(it, 0), in1=BK(w[2]))
                v = vector.tensor_mul(out=hpiece(it, 1), in0=hpiece(it, 1), in1=BK(w[3]))
                INC(v, "c_h", (tc, it))

            def rv2_ot(tc, ot):
                j = tc * NDT + ot
                c0, c1 = pool3(j)
                WAIT(vector, "c_mmd", (tc, ot, 2))
                WAIT(vector, "d_x2r", (tc, ot))
                r2t = AR[1][:, 0:1024]
                v = vector.tensor_add(out=r2t[:, 0:512], in0=r2t[:, 0:512], in1=BK(c0))
                INC(v, "c_rv2", (tc, ot, 0))
                v = vector.tensor_add(out=r2t[:, 512:1024], in0=r2t[:, 512:1024], in1=BK(c1))
                INC(v, "c_rv2", (tc, ot, 1))

            def nv2_ot(tc, ot):
                WAIT(vector, "d_r2r", (tc, ot))
                if ot == 0:
                    WAIT(vector, "c_mr2s", tc)
                slt = AR[1][:, 1024:2048]
                vector.tensor_sub(out=slt, in0=slt, in1=MB[:, 1024 * tc:1024 * (tc + 1)])
                v = vector.tensor_mul(out=slt, in0=slt, in1=RSB[:, 1024 * tc:1024 * (tc + 1)])
                INC(v, "c_nv2", (tc, ot))

            for it in range(NIT):
                hmul(0, it)
                if it < NDT:
                    nv_ct(1, it)
            for ot in range(NDT):
                rv2_ot(0, ot)
            WAIT(vector, "c_mst2", (0, NDT - 1))
            WAIT(vector, "c_mmp", (1, NDT - 1))
            mchain(slice(0, 1024), 3, 5, T1M2, "d_r2w", (0, NDT - 1), "c_mr2v", 0)
            for it in range(NIT):
                hmul(1, it)
                if it < NDT:
                    nv2_ot(0, it)
            for ot in range(NDT):
                rv2_ot(1, ot)
            WAIT(vector, "c_mst2", (1, NDT - 1))
            mchain(slice(1024, 2048), 3, 5, T1M2, "d_r2w", (1, NDT - 1), "c_mr2v", 1,
                   pre=[("d_out", (0, NDT - 1))])
            for ot in range(NDT):
                nv2_ot(1, ot)

        @block.gpsimd
        def _(gpsimd):
            CURENG[0] = "gpsimd"
            st = {"conv": 0, "dma": 0, "out": 0}

            def idx_dma():
                mi = st["dma"]
                m, dt = DQ[mi]
                if mi >= 1:
                    WAIT(gpsimd, "c_idxc", DQ[mi - 1])
                if m == "d":
                    ri, cj = dt
                    r0, rn = _d_rows[ri]
                    c0, cn = _d_cols[cj]
                    d = gpsimd.dma_start(out=IDX32[0:rn, 0:cn], in_=didx[r0:r0 + rn, c0:c0 + cn])
                else:
                    srct = gidx if m == "g" else uidx
                    d = gpsimd.dma_start(out=IDX32[:, :], in_=srct[128 * dt:128 * (dt + 1), :])
                INC(d, "d_idx", (m, dt), 16)
                st["dma"] += 1

            def idx_conv():
                mi = st["conv"]
                m, dt = DQ[mi]
                WAIT(gpsimd, "d_idx", (m, dt))
                if mi >= 2:
                    WAIT(gpsimd, "c_deq", DQ[mi - 2])
                if m == "d":
                    rn = _d_rows[dt[0]][1]
                    cn = _d_cols[dt[1]][1]
                    c = gpsimd.tensor_copy(out=IDXBF[mi % 2][0:rn, 0:cn], in_=IDX32[0:rn, 0:cn])
                else:
                    c = gpsimd.tensor_copy(out=IDXBF[mi % 2][:, :], in_=IDX32[:, :])
                INC(c, "c_idxc", (m, dt))
                st["conv"] += 1

            def wdeq_dma():
                mi = st["out"]
                m, dt = DQ[mi]
                WAIT(gpsimd, "c_deq", (m, dt))
                if m == "d":
                    ri, cj = dt
                    r0, rn = _d_rows[ri]
                    c0, cn = _d_cols[cj]
                    d = gpsimd.dma_start(out=ccd_in[r0:r0 + rn, c0:c0 + cn],
                                         in_=WDQ[mi % 2][0:rn, 0:cn])
                else:
                    co = 0 if m == "g" else ISH
                    d = gpsimd.dma_start(out=ccgu_in[128 * dt:128 * (dt + 1), co:co + ISH],
                                         in_=WDQ[mi % 2][:, :])
                INC(d, "d_wdeq", (m, dt), 16)
                st["out"] += 1

            def deq_step():
                if st["dma"] == 0:
                    idx_dma()
                if st["conv"] < NDQ:
                    idx_conv()
                if st["dma"] < NDQ:
                    idx_dma()
                if st["out"] < st["conv"] - 1:
                    wdeq_dma()

            while st["conv"] < 38:
                deq_step()
            for ii, (h, qc, kt) in enumerate(INST):
                qlo, qn, rel, diag = inst_geom(h, qc, kt)
                if kt == lastkt(qc) and (st["conv"] < NDQ or st["out"] < st["conv"] - 1):
                    deq_step()
            while st["conv"] < NDQ:
                deq_step()
            while st["out"] < NDQ:
                wdeq_dma()
            WAIT(gpsimd, "d_wdeq", DQ[2 * NDT - 1])
            cc = gpsimd.collective_compute(
                "AllGather", ALU.bypass, replica_groups=[list(range(NCORES))],
                ins=[ccgu_in[:, :]], outs=[ccgu_out[:, :]])
            INC(cc, "d_ccg", 0, 1)
            WAIT(gpsimd, "d_wdeq", DQ[NDQ - 1])
            cc = gpsimd.collective_compute(
                "AllGather", ALU.bypass, replica_groups=[list(range(NCORES))],
                ins=[ccd_in[:, :]], outs=[ccd_out[:, :]])
            INC(cc, "d_ccd", 0, 1)
            for tc in range(NTC):
                for it in range(NIT):
                    for m in ("g", "u"):
                        fbk = FBIDX[("gu", tc, it, m)]
                        if fbk >= 2:
                            cs, ck = fb_consume(FBSEQ[fbk - 2])
                            WAIT(gpsimd, cs, ck)
                        if tc == 0 and it == 0:
                            gpsimd.wait_ge(S["d_ccg"], 1)
                            if m == "g":
                                WTOT(gpsimd, "c_pv")
                                WTOT(gpsimd, "c_ev")
                                WTOT(gpsimd, "d_wdeq")
                        off = 0 if m == "g" else ISH
                        srcr = ccgu_out.rearrange("(r t p) c -> r p t c", r=NCORES, p=128)
                        dstr = FBv[fbk % 2].rearrange("p (t c) -> p t c", t=NDT)
                        parts = gu_split(it)
                        for pi2, (blk, c0, c1, doff) in enumerate(parts):
                            d = gpsimd.dma_start(out=dstr[:, :, doff:doff + (c1 - c0)],
                                                 in_=srcr[blk][:, :, off + c0:off + c1])
                            if pi2 < len(parts) - 1:
                                RAWINC(d, "d_gu", 16)
                            else:
                                INCF(d, "d_gu", (tc, it, m), 16)
                for ot in range(NDT):
                    for gi, (g0, gn) in enumerate(DGRP):
                        fbk = FBIDX[("wd", tc, ot, gi)]
                        cs, ck = fb_consume(FBSEQ[fbk - 2])
                        WAIT(gpsimd, cs, ck)
                        if tc == 0 and ot == 0 and gi == 0:
                            gpsimd.wait_ge(S["d_ccd"], 1)
                        srcr = ccd_out.rearrange("(t p) c -> p t c", p=128)
                        d = gpsimd.dma_start(
                            out=FBv[fbk % 2][:, 0:128 * gn].rearrange("p (t c) -> p t c", t=gn),
                            in_=srcr[:, g0:g0 + gn, 128 * ot:128 * (ot + 1)])
                        INC(d, "d_wd", (tc, ot, gi), 16)

    for s in SEMS:
        assert emitted[s] == L.tot[s], (s, emitted[s], L.tot[s])
    return nc


def _rope_tables():
    dd = np.arange(0, HD, 2, dtype=np.float64) / HD
    inv = 1.0 / np.power(10000.0, dd)
    t = np.arange(T, dtype=np.float64)
    fr = inv[:, None] * t[None, :]
    cosT = np.concatenate([np.cos(fr), np.cos(fr)], axis=0).astype(np.float32)
    sinT = np.concatenate([-np.sin(fr), np.sin(fr)], axis=0).astype(np.float32)
    return np.ascontiguousarray(cosT), np.ascontiguousarray(sinT)


def _consts():
    kl = np.arange(128)[:, None]
    ql = np.arange(128)[None, :]
    mask01 = (ql >= kl).astype(ml_dtypes.bfloat16)
    ones = np.ones((128, 128), dtype=ml_dtypes.bfloat16)
    ident = np.eye(128, dtype=ml_dtypes.bfloat16)
    oon = np.full((128, 128), 1.0 / D, dtype=ml_dtypes.bfloat16)
    return np.ascontiguousarray(np.concatenate([mask01, ones, ident, oon], axis=1))


_NC_CACHE = None
TRACE = None
DEBUG_DUMP = False


def kernel(L0_attn_x, L0_attn_Wq, L0_attn_Wk, L0_attn_Wv, L0_attn_Wproj,
           L0_ffn_lut, L0_ffn_gate_idx, L0_ffn_up_idx, L0_ffn_down_idx):
    global _NC_CACHE
    x = np.asarray(L0_attn_x, dtype=np.float32)
    def _wlay(w):
        w = np.asarray(w, dtype=np.float32)
        nb = w.shape[1] // 128
        w4 = w.reshape(NDT, 128, nb, 128).transpose(1, 2, 0, 3)
        return np.ascontiguousarray(w4.reshape(128, nb * NDT * 128))

    wqh = _wlay(L0_attn_Wq)
    wkh = _wlay(L0_attn_Wk)
    wvh = _wlay(L0_attn_Wv)
    wph = _wlay(L0_attn_Wproj)
    lut = np.asarray(L0_ffn_lut, dtype=np.float32)
    gi = np.asarray(L0_ffn_gate_idx, dtype=np.int32)
    ui = np.asarray(L0_ffn_up_idx, dtype=np.int32)
    di = np.asarray(L0_ffn_down_idx, dtype=np.int32)

    lutr = np.ascontiguousarray(np.broadcast_to(lut[None, :], (128, LUTN)))
    cosT, sinT = _rope_tables()
    cst = _consts()
    giT = np.ascontiguousarray(gi.T)
    uiT = np.ascontiguousarray(ui.T)
    diT = np.ascontiguousarray(di.T)

    if _NC_CACHE is None:
        _NC_CACHE = build_nc()
    nc = _NC_CACHE

    in_maps = []
    for b in range(NCORES):
        in_maps.append({
            "xT": np.ascontiguousarray(x[b].T),
            "wq": wqh, "wk": wkh, "wv": wvh, "wp": wph,
            "lutr": lutr,
            "gidx": np.ascontiguousarray(giT[:, ISH * b:ISH * (b + 1)]),
            "uidx": np.ascontiguousarray(uiT[:, ISH * b:ISH * (b + 1)]),
            "didx": np.ascontiguousarray(diT[ISH * b:ISH * (b + 1), :]),
            "cosd": cosT, "sind": sinT, "cstd": cst,
            "actc": np.ascontiguousarray(np.broadcast_to(
                np.array([-EXPC, SCALE, -1.0, 0.0], dtype=np.float32)[None, :], (128, 4))),
        })
    res = run_bass_kernel_spmd(nc, in_maps, core_ids=list(range(NCORES)))
    out = np.empty((NCORES, T, D), dtype=np.float32)
    for b in range(NCORES):
        out[b] = res.results[b]["outT"].T
    return out
